# version 03
# baseline (speedup 1.0000x reference)
import contextlib
import math

import numpy as np
import ml_dtypes

import concourse.bass as bass
import concourse.mybir as mybir
from concourse.bass_utils import run_bass_kernel_spmd

F32 = mybir.dt.float32
BF16 = mybir.dt.bfloat16
AF = mybir.ActivationFunctionType
ALU = mybir.AluOpType
AX = mybir.AxisListType.X

NCORES = 8
D = 1024
KC = 8
HEADS = 8
DH = 64
INNER = 512
NEXP = 8
ICH = 11
MASKNEG = -1.25e29

BF = ml_dtypes.bfloat16


def _bucket_of_d(S):
    d = np.arange(S)
    n = np.maximum(d, 0)
    lg = np.log(np.maximum(n, 1).astype(np.float32) / np.float32(16.0))
    val = np.int32(16) + (lg / np.float32(math.log(8.0)) * np.float32(16.0)).astype(np.int32)
    val = np.minimum(val, 31)
    return np.where(n < 16, n, val).astype(np.int64)


def _build_lr(rel_emb, S):
    P0 = S + 127
    lpad = np.zeros((HEADS, S + 254), np.float32)
    lpad[:, :127] = MASKNEG
    buck = _bucket_of_d(S)
    lpad[:, 127:127 + S] = rel_emb.astype(np.float32).T[:, buck]
    v = np.arange(P0)
    p = np.arange(128)
    idx = (P0 - 1) - v[None, :] + p[:, None]
    lr = lpad[:, idx]
    pad = (-P0) % 16
    if pad:
        lr = np.concatenate([lr, np.zeros((HEADS, 128, pad), np.float32)], axis=2)
    return np.ascontiguousarray(lr), P0 + pad


def _img_lhsT(w):
    K, Mt = w.shape
    kc = K // 128
    return np.ascontiguousarray(
        w.reshape(kc, 128, Mt).transpose(1, 0, 2).reshape(128, kc * Mt).astype(np.float32))


def _img_gu(g_rows, u_rows):
    out = np.empty((128, ICH * 2048), np.float32)
    for gu, W in enumerate((g_rows, u_rows)):
        a = W.T.reshape(KC, 128, ICH, 128).transpose(1, 2, 0, 3)
        for ic in range(ICH):
            base = ic * 2048 + gu * 1024
            out[:, base:base + 1024] = a[:, ic].reshape(128, 1024)
    return np.ascontiguousarray(out)


def _img_down(dn):
    a = dn.T.reshape(ICH, 128, KC, 128).transpose(1, 0, 2, 3)
    return np.ascontiguousarray(a.reshape(128, ICH * 1024).astype(np.float32))


class Ctr:
    def __init__(self, es, nc, name):
        self.sem = es.enter_context(nc.semaphore(name))
        self.name = name
        self.v = 0


class Stream:
    def __init__(self, name):
        self.name = name
        self.ops = []
        self.last_wait = {}

    def wait(self, ctr, v=None):
        if v is None:
            v = ctr.v
        if v <= 0 or self.last_wait.get(ctr.name, -1) >= v:
            return
        self.last_wait[ctr.name] = v
        self.ops.append(lambda e, s=ctr.sem, vv=v: e.wait_ge(s, vv))

    def op(self, fn, inc=None, n=1):
        if inc is not None:
            inc.v += n
            self.ops.append(lambda e, f=fn, s=inc.sem, nn=n: f(e).then_inc(s, nn))
            return inc.v
        self.ops.append(fn)
        return None


class Phase:
    def __init__(self):
        self.t = Stream("t")
        self.v = Stream("v")
        self.s = Stream("s")
        self.y = Stream("y")
        self.g = Stream("g")

    def emit(self, block):
        for st, reg in ((self.g, block.gpsimd), (self.y, block.sync), (self.t, block.tensor),
                        (self.v, block.vector), (self.s, block.scalar)):
            if st.ops:
                ops = st.ops
                reg(lambda e, ops=ops: [f(e) for f in ops])


class Pool:
    def __init__(self, slots):
        self.slots = [{"t": s, "free": None} for s in slots]
        self.i = 0

    def acquire(self, stream):
        s = self.slots[self.i % len(self.slots)]
        self.i += 1
        if s["free"] is not None:
            stream.wait(*s["free"])
        return s


class _StopBuild(Exception):
    pass


def build_program(S, BPC, LRP, stop=None):
    nc = bass.Bass("TRN2", target_bir_lowering=False)
    NT = S // 128
    NTN = S // 512
    TH = S // 2
    THN = TH // 512
    THC = TH // 128

    ap = {}
    def din(name, shape, dt=F32):
        ap[name] = nc.dram_tensor(name, shape, dt, kind="ExternalInput").ap()
    din("x", [BPC, S, D])
    din("wq_i", [128, KC * INNER])
    din("wkv_i", [128, 8 * 1024])
    din("wo_i", [128, 4 * 1024])
    din("bo_i", [128, 8])
    din("gate_i", [128, KC * NEXP])
    din("seg_i", [128, KC])
    din("lr_i", [HEADS, 128, LRP])
    din("wgu_i", [10, 128, ICH * 2048])
    din("wdn_i", [10, 128, ICH * 1024])
    din("cid_f", [128, 128])
    din("ceps", [128, 1])
    din("csel", [8, 8 * 128], BF16)
    din("cmisc", [128, 3 * 128], BF16)
    y_out = nc.dram_tensor("y", [BPC, S, D], F32, kind="ExternalOutput").ap()

    es = contextlib.ExitStack()
    try:
      with es:
        cT = Ctr(es, nc, "cT")
        cV = Ctr(es, nc, "cV")
        cS = Ctr(es, nc, "cS")
        dW = Ctr(es, nc, "dW")
        dIO = Ctr(es, nc, "dIO")
        dOUT = Ctr(es, nc, "dOUT")
        ALLC = [cT, cV, cS, dW, dIO, dOUT]

        minrem = [10 ** 9]

        def sba(stack, n, sh, dt):
            t = stack.enter_context(nc.sbuf_tensor(n, sh, dt))
            minrem[0] = min(minrem[0], nc.sbuf_bytes_remaining)
            return t

        def psa(stack, n, sh, dt=F32):
            return stack.enter_context(nc.psum_tensor(n, sh, dt))

        cid_f = sba(es, "cid_f_sb", [128, 128], F32)
        cmisc = sba(es, "cmisc_sb", [128, 3 * 128], BF16)
        id_bf = cmisc[:, 0:128]
        eye8_bf = cmisc[:, 128:256]
        ones_bf = cmisc[:, 256:384]
        bo_sb = sba(es, "bo_sb0", [128, 8], F32)
        ceps_sb = sba(es, "ceps_sb", [128, 1], F32)
        csel_sb = sba(es, "csel_sb", [8, 8 * 128], BF16)
        nc.const_aps.register(F32, 1e-5, ceps_sb[:, 0:1])

        block = es.enter_context(nc.Block())

        def new_phase():
            P = Phase()
            for st in (P.t, P.v, P.s, P.y, P.g):
                for c in ALLC:
                    st.wait(c, c.v)
            return P

        def dump_and_stop(label, bufs):
            if stop != label:
                return
            Pd = Phase()
            for c in ALLC:
                Pd.y.wait(c, c.v)
            for name, a in bufs.items():
                t = nc.dram_tensor("tap_" + name, list(a.shape), a.dtype,
                                   kind="ExternalOutput").ap()
                Pd.y.op(lambda e, t=t, a=a: e.dma_start(out=t, in_=a), dOUT, 16)
            Pd.y.wait(dOUT, dOUT.v)
            Pd.emit(block)
            raise _StopBuild()

        ph = Phase()
        ph.g.op(lambda e: e.dma_start(out=cmisc[:], in_=ap["cmisc"]), dW, 16)
        ph.y.op(lambda e: e.dma_start(out=cid_f[:], in_=ap["cid_f"]), dIO, 16)
        ph.y.op(lambda e: e.dma_start(out=bo_sb[:], in_=ap["bo_i"]), dIO, 16)
        ph.y.op(lambda e: e.dma_start(out=ceps_sb[:], in_=ap["ceps"]), dIO, 16)
        ph.g.op(lambda e: e.dma_start(out=csel_sb[:], in_=ap["csel"]), dW, 16)
        ph.emit(block)

        def ln_chunk(P, pool, stat, src, guards):
            s1 = pool.acquire(P.t)
            s2 = pool.acquire(P.t)
            for kc in range(KC):
                a, ws, gslot = src(kc)
                for w in ws:
                    P.v.wait(*w)
                sq = stat["sq"][kc % 2]
                if stat["sq_free"][kc % 2] is not None:
                    P.v.wait(*stat["sq_free"][kc % 2])
                P.v.op(lambda e, a=a, q=sq: e.tensor_tensor(out=q[:], in0=a, in1=a, op=ALU.mult), cV)
                P.t.wait(cV)
                for w in ws:
                    P.t.wait(*w)
                P.t.op(lambda e, a=a, st=(kc == 0), sp=(kc == KC - 1), ps=s1["t"]:
                       e.matmul(ps[:, 0:512], ones_bf, a, start=st, stop=sp), cT)
                if gslot is not None:
                    guards[gslot] = (cT, cT.v)
                P.t.op(lambda e, q=sq, st=(kc == 0), sp=(kc == KC - 1), ps=s2["t"]:
                       e.matmul(ps[:, 0:512], ones_bf, q[:], start=st, stop=sp), cT)
                stat["sq_free"][kc % 2] = (cT, cT.v)
            P.v.wait(cT)
            P.v.op(lambda e, ps=s1["t"]: e.tensor_scalar(out=stat["mean"][:], in0=ps[:, 0:512],
                                                         scalar1=1.0 / D, scalar2=None, op0=ALU.mult))
            P.v.op(lambda e: e.tensor_tensor(out=stat["var"][:], in0=stat["mean"][:],
                                             in1=stat["mean"][:], op=ALU.mult))
            P.v.op(lambda e, ps=s2["t"]: e.scalar_tensor_tensor(out=stat["var"][:], in0=ps[:, 0:512],
                                                                scalar=1.0 / D, in1=stat["var"][:],
                                                                op0=ALU.mult, op1=ALU.subtract), cV)
            s1["free"] = (cV, cV.v)
            s2["free"] = (cV, cV.v)
            P.s.wait(cV)
            P.s.op(lambda e: e.activation(out=stat["var"][:], in_=stat["var"][:], func=AF.Sqrt,
                                          bias=1e-5, scale=1.0), cS)
            P.v.wait(cS)
            P.v.op(lambda e: e.reciprocal(stat["rinv"][:], stat["var"][:]), cV)
            P.v.wait(cV)
            P.v.op(lambda e: e.tensor_tensor(out=stat["mrinv"][:], in0=stat["mean"][:],
                                             in1=stat["rinv"][:], op=ALU.mult), cV)
            P.v.wait(cV)
            return cV.v

        def mkstat(stack, tag, with_xbf):
            st = {
                "mean": sba(stack, f"mean{tag}", [128, 512], F32),
                "var": sba(stack, f"var{tag}", [128, 512], F32),
                "rinv": sba(stack, f"rinv{tag}", [128, 512], F32),
                "mrinv": sba(stack, f"mrinv{tag}", [128, 512], F32),
                "sq": [sba(stack, f"sq{tag}{i}", [128, 512], BF16) for i in range(2)],
                "sq_free": [None, None],
                "t1": sba(stack, f"t1{tag}", [128, 512], F32),
            }
            if with_xbf:
                st["xbf"] = [sba(stack, f"xbf{tag}{i}", [128, 512], BF16) for i in range(2)]
            return st

        for b in range(BPC):
            esb = contextlib.ExitStack()
            with esb:
                resid = sba(esb, f"resid{b}", [128, KC * S], F32)
                resid3 = resid[:].rearrange("p (kc t) -> p kc t", kc=KC)

                kqv = contextlib.ExitStack()
                with kqv:
                    qT = sba(kqv, f"qT{b}", [128, 4 * S], BF16)
                    kT = sba(kqv, f"kT{b}", [128, 4 * S], BF16)
                    vtok = sba(kqv, f"vtok{b}", [128, NT * 512], BF16)
                    av_tok = sba(kqv, f"avtok{b}", [128, NT * 512], BF16)

                    b1 = contextlib.ExitStack()
                    with b1:
                        wq_bf = sba(b1, f"wq{b}", [128, KC * INNER], BF16)
                        wkv_bf = sba(b1, f"wkv{b}", [128, 8 * 1024], BF16)
                        lnx2 = [sba(b1, f"lnx{b}{i}", [128, KC * 512], BF16) for i in range(2)]
                        xin = [sba(b1, f"xin{b}{i}", [128, D], F32) for i in range(2)]
                        stat = mkstat(b1, f"a{b}", True)
                        pb1 = contextlib.ExitStack()
                        with pb1:
                            pX = Pool([psa(pb1, f"pX{b}{i}", [128, 512], F32) for i in range(2)])
                            pSt = Pool([psa(pb1, f"pSt{b}{i}", [128, 512], F32) for i in range(2)])
                            pQ = Pool([psa(pb1, f"pQ{b}{i}", [128, 512], F32) for i in range(4)])

                            P = new_phase()
                            P.g.op(lambda e: e.dma_start(out=wq_bf[:], in_=ap["wq_i"]), dW, 16)
                            P.g.op(lambda e: e.dma_start(out=wkv_bf[:], in_=ap["wkv_i"]), dW, 16)
                            wqkv_w = dW.v

                            xin_free = [None, None]
                            for tc in range(NT):
                                xi = xin[tc % 2]
                                if xin_free[tc % 2] is not None:
                                    P.y.wait(*xin_free[tc % 2])
                                P.y.op(lambda e, xi=xi, tc=tc:
                                       e.dma_start(out=xi[:], in_=ap["x"][b, tc * 128:(tc + 1) * 128, :]),
                                       dIO, 16)
                                P.t.wait(dIO)
                                for g2 in range(2):
                                    s = pX.acquire(P.t)
                                    for dc in range(4):
                                        dcg = g2 * 4 + dc
                                        P.t.op(lambda e, xi=xi, dcg=dcg, dc=dc, ps=s["t"]:
                                               e.transpose(ps[:, dc * 128:(dc + 1) * 128],
                                                           xi[:, dcg * 128:(dcg + 1) * 128], cid_f[:]),
                                               cT if dc == 3 else None)
                                    xin_free[tc % 2] = (cT, cT.v)
                                    P.v.wait(cT)
                                    P.v.op(lambda e, tc=tc, g2=g2, ps=s["t"]:
                                           e.tensor_copy(resid3[:, g2 * 4:(g2 + 1) * 4, tc * 128:(tc + 1) * 128],
                                                         ps[:].rearrange("p (g t) -> p g t", g=4)), cV)
                                    s["free"] = (cV, cV.v)
                            resid_ready = cV.v

                            lnx_free = [None, None]
                            xbf_guards = [None, None]
                            P.t.wait(dW, wqkv_w)
                            for tn in range(NTN):
                                def src1(kc, tn=tn):
                                    xb = stat["xbf"][kc % 2]
                                    if xbf_guards[kc % 2] is not None:
                                        P.v.wait(*xbf_guards[kc % 2])
                                    P.v.wait(cV, resid_ready)
                                    P.v.op(lambda e, kc=kc, xb=xb, tn=tn: e.tensor_copy(
                                        xb[:], resid[:, kc * S + tn * 512: kc * S + tn * 512 + 512]), cV)
                                    return xb[:], [(cV, cV.v)], kc % 2
                                stats_at = ln_chunk(P, pSt, stat, src1, xbf_guards)

                                lnx = lnx2[tn % 2]
                                P.v.wait(cV, stats_at)
                                if lnx_free[tn % 2] is not None:
                                    P.v.wait(*lnx_free[tn % 2])
                                for kc in range(KC):
                                    P.v.op(lambda e, kc=kc, tn=tn: e.tensor_tensor(
                                        out=stat["t1"][:],
                                        in0=resid[:, kc * S + tn * 512: kc * S + tn * 512 + 512],
                                        in1=stat["rinv"][:], op=ALU.mult))
                                    P.v.op(lambda e, kc=kc, lnx=lnx: e.tensor_tensor(
                                        out=lnx[:, kc * 512:(kc + 1) * 512], in0=stat["t1"][:],
                                        in1=stat["mrinv"][:], op=ALU.subtract), cV)
                                lnx_at = cV.v

                                for qk in range(2):
                                    for mc in range(4):
                                        s = pQ.acquire(P.t)
                                        P.t.wait(cV, lnx_at)
                                        for kc in range(KC):
                                            if qk == 0:
                                                lhs = wq_bf[:, kc * 512 + mc * 128: kc * 512 + (mc + 1) * 128]
                                            else:
                                                lhs = wkv_bf[:, kc * 1024 + mc * 128: kc * 1024 + (mc + 1) * 128]
                                            P.t.op(lambda e, lhs=lhs, kc=kc, lnx=lnx, ps=s["t"]:
                                                   e.matmul(ps[:], lhs, lnx[:, kc * 512:(kc + 1) * 512],
                                                            start=(kc == 0), stop=(kc == KC - 1)),
                                                   cT if kc == KC - 1 else None)
                                        P.s.wait(cT)
                                        if qk == 0:
                                            P.s.op(lambda e, mc=mc, tn=tn, ps=s["t"]:
                                                   e.mul(qT[:, mc * S + tn * 512: mc * S + tn * 512 + 512],
                                                         ps[:], 0.125), cS)
                                        else:
                                            P.s.op(lambda e, mc=mc, tn=tn, ps=s["t"]:
                                                   e.copy(kT[:, mc * S + tn * 512: mc * S + tn * 512 + 512],
                                                          ps[:]), cS)
                                        s["free"] = (cS, cS.v)
                                for tcl in range(4):
                                    s = pQ.acquire(P.t)
                                    for kc in range(KC):
                                        P.t.op(lambda e, kc=kc, tcl=tcl, lnx=lnx, ps=s["t"]:
                                               e.matmul(ps[:],
                                                        lnx[:, kc * 512 + tcl * 128: kc * 512 + (tcl + 1) * 128],
                                                        wkv_bf[:, kc * 1024 + 512: kc * 1024 + 1024],
                                                        start=(kc == 0), stop=(kc == KC - 1)),
                                               cT if kc == KC - 1 else None)
                                    lnx_free[tn % 2] = (cT, cT.v)
                                    P.s.wait(cT)
                                    tcg = tn * 4 + tcl
                                    P.s.op(lambda e, tcg=tcg, ps=s["t"]:
                                           e.copy(vtok[:, tcg * 512:(tcg + 1) * 512], ps[:]), cS)
                                    s["free"] = (cS, cS.v)
                            P.emit(block)
                            if b == 0:
                                dump_and_stop("b1", {"resid": resid[:], "qT": qT[:], "kT": kT[:],
                                                     "vtok": vtok[:], "lnx0": lnx2[0][:], "lnx1": lnx2[1][:]})

                    b2 = contextlib.ExitStack()
                    with b2:
                        lr_bf = [sba(b2, f"lr{b}{i}", [128, LRP], BF16) for i in range(2)]
                        aexp = [sba(b2, f"aexp{b}{i}", [128, S], BF16) for i in range(2)]
                        attnT = sba(b2, f"attnT{b}", [128, S], BF16)
                        nm4 = sba(b2, f"nm4{b}", [128, 4], F32)
                        nmax = sba(b2, f"nmax{b}", [128, 1], F32)
                        sums = sba(b2, f"sums{b}", [128, 1], F32)
                        rnv = [sba(b2, f"rnv{b}{i}", [128, 1], F32) for i in range(2)]
                        pb2 = contextlib.ExitStack()
                        with pb2:
                            pSc = psa(pb2, f"pSc{b}", [128, min(S, 2048)], F32)
                            pTr = Pool([psa(pb2, f"pTr{b}{i}", [128, 1024], BF16) for i in range(2)])
                            pAV = Pool([psa(pb2, f"pAV{b}{i}", [128, 64], F32) for i in range(2)])

                            P = new_phase()
                            lr_w = [None, None]
                            lr_free = [None, None]
                            exp_done = [0]
                            aexp_free = [None, None]
                            pend = [None]

                            def flush(P=P):
                                fh, fti, fexp_at, fpar = pend[0]
                                ng = (fti + 1 + 7) // 8
                                for g in range(ng):
                                    j0 = g * 8
                                    j1 = min(j0 + 8, fti + 1)
                                    s = pTr.acquire(P.t)
                                    P.t.wait(cS, fexp_at)
                                    for jc in range(j0, j1):
                                        P.t.op(lambda e, jc=jc, j0=j0, fpar=fpar, ps=s["t"]:
                                               e.transpose(ps[:, (jc - j0) * 128:(jc - j0 + 1) * 128],
                                                           aexp[fpar][:, jc * 128:(jc + 1) * 128], id_bf),
                                               cT if jc == j1 - 1 else None)
                                    if g % 2 == 0:
                                        P.s.wait(cT)
                                        P.s.op(lambda e, j0=j0, j1=j1, ps=s["t"]:
                                               e.copy(attnT[:, j0 * 128: j1 * 128],
                                                      ps[:, 0:(j1 - j0) * 128]), cS)
                                        s["free"] = (cS, cS.v)
                                        at_s = cS.v
                                    else:
                                        P.v.wait(cT)
                                        P.v.op(lambda e, j0=j0, j1=j1, ps=s["t"]:
                                               e.tensor_copy(attnT[:, j0 * 128: j1 * 128],
                                                             ps[:, 0:(j1 - j0) * 128]), cV)
                                        s["free"] = (cV, cV.v)
                                aexp_free[fpar] = (cT, cT.v)
                                sav = pAV.acquire(P.t)
                                P.t.wait(cS)
                                P.t.wait(cV)
                                for jc in range(fti + 1):
                                    P.t.op(lambda e, jc=jc, fh=fh, ps=sav["t"]:
                                           e.matmul(ps[:],
                                                    attnT[:, jc * 128:(jc + 1) * 128],
                                                    vtok[:, jc * 512 + fh * 64: jc * 512 + fh * 64 + 64],
                                                    start=(jc == 0), stop=(jc == fti)),
                                           cT if jc == fti else None)
                                P.v.wait(cT)
                                P.v.op(lambda e, fh=fh, fti=fti, fpar=fpar, ps=sav["t"]:
                                       e.tensor_scalar(out=av_tok[:, fti * 512 + fh * 64: fti * 512 + fh * 64 + 64],
                                                       in0=ps[:], scalar1=rnv[fpar][:], scalar2=None,
                                                       op0=ALU.mult), cV)
                                sav["free"] = (cV, cV.v)
                                pend[0] = None

                            for h in range(HEADS):
                                if lr_free[h % 2] is not None:
                                    P.g.wait(*lr_free[h % 2])
                                P.g.op(lambda e, h=h: e.dma_start(out=lr_bf[h % 2][:], in_=ap["lr_i"][h]), dW, 16)
                                lr_w[h % 2] = dW.v
                                qrow = (h % 2) * 64
                                qcol = (h // 2) * S
                                for ti in range(NT):
                                    Nj = (ti + 1) * 128
                                    P.t.wait(dW, lr_w[h % 2])
                                    P.t.wait(cS, exp_done[0])
                                    v0 = (S - 1) - ti * 128
                                    nch = (Nj + 511) // 512
                                    chunk_t = []
                                    for c in range(nch):
                                        cw = min(512, Nj - c * 512)
                                        P.t.op(lambda e, c=c, cw=cw, v0=v0, par=h % 2:
                                               e.matmul(pSc[:, c * 512: c * 512 + cw], eye8_bf,
                                                        lr_bf[par][:, v0 + c * 512: v0 + c * 512 + cw],
                                                        start=True, stop=False))
                                        P.t.op(lambda e, ti=ti, c=c, cw=cw, qrow=qrow, qcol=qcol:
                                               e.matmul(pSc[:, c * 512: c * 512 + cw],
                                                        qT[qrow:qrow + 64, qcol + ti * 128: qcol + (ti + 1) * 128],
                                                        kT[qrow:qrow + 64, qcol + c * 512: qcol + c * 512 + cw],
                                                        start=False, stop=True), cT)
                                        chunk_t.append(cT.v)
                                    scores_t = cT.v
                                    P.v.wait(cS, exp_done[0])
                                    for c in range(nch):
                                        cw = min(512, Nj - c * 512)
                                        P.v.wait(cT, chunk_t[c])
                                        P.v.op(lambda e, c=c, cw=cw:
                                               e.reduce_max(nm4[:, c:c + 1], pSc[:, c * 512: c * 512 + cw],
                                                            axis=AX, negate=True),
                                               cV if c == nch - 1 else None)
                                    P.v.wait(cV)
                                    if nch > 1:
                                        P.v.op(lambda e, nch=nch:
                                               e.tensor_reduce(nmax[:], nm4[:, 0:nch], axis=AX, op=ALU.min),
                                               cV)
                                        P.v.wait(cV)
                                        bias_ap = nmax
                                    else:
                                        bias_ap = None
                                    max_v = cV.v
                                    if aexp_free[ti % 2] is not None:
                                        P.s.wait(*aexp_free[ti % 2])
                                    P.s.wait(cV, max_v)
                                    P.s.op(lambda e, Nj=Nj, par=ti % 2, ba=bias_ap:
                                           e.activation(out=aexp[par][:, 0:Nj], in_=pSc[:, 0:Nj], func=AF.Exp,
                                                        bias=(ba[:] if ba is not None else nm4[:, 0:1]),
                                                        scale=1.0, accum_out=sums[:]), cS)
                                    exp_done[0] = cS.v
                                    P.v.wait(cS, exp_done[0])
                                    P.v.op(lambda e, par=ti % 2: e.reciprocal(rnv[par][:], sums[:]), cV)
                                    P.v.wait(cV)
                                    exp_at = exp_done[0]
                                    if pend[0] is not None:
                                        flush()
                                    pend[0] = (h, ti, exp_at, ti % 2)
                                    if b == 0 and h == 0 and ti == 1 and stop == "b2s":
                                        flush()
                                        P.emit(block)
                                        dump_and_stop("b2s", {
                                            "aexp0": aexp[0][:],
                                            "aexp1": aexp[1][:],
                                            "attnT": attnT[:], "avtok": av_tok[:],
                                            "nmax": nmax[:], "sums": sums[:]})
                                lr_free[h % 2] = (cT, cT.v)
                            flush()
                            lr_free[(HEADS - 1) % 2] = (cT, cT.v)
                            P.emit(block)
                            if b == 0:
                                dump_and_stop("b2", {"avtok": av_tok[:], "attnT": attnT[:],
                                                     "aexp0": aexp[0][:],
                                                     "qT": qT[:], "kT": kT[:], "vtok": vtok[:]})

                    b3 = contextlib.ExitStack()
                    with b3:
                        wo_bf = sba(b3, f"wo{b}", [128, 4 * 1024], BF16)
                        avT = sba(b3, f"avT{b}", [128, 4 * S], BF16)
                        ao = sba(b3, f"ao{b}", [128, KC * S], BF16)
                        stat3 = mkstat(b3, f"c{b}", False)
                        pb3 = contextlib.ExitStack()
                        with pb3:
                            pW = Pool([psa(pb3, f"pW{b}{i}", [128, 512], F32) for i in range(2)])
                            pSt3 = Pool([psa(pb3, f"pSt3{b}{i}", [128, 512], F32) for i in range(2)])
                            pTrB = Pool([psa(pb3, f"pTrB{b}{i}", [128, 512], BF16) for i in range(2)])
                            P = new_phase()
                            P.g.op(lambda e: e.dma_start(out=wo_bf[:], in_=ap["wo_i"]), dW, 16)
                            wo_w = dW.v
                            avT3 = avT[:].rearrange("p (kc t) -> p kc t", kc=4)
                            for ti in range(NT):
                                s = pTrB.acquire(P.t)
                                for fc in range(4):
                                    P.t.op(lambda e, ti=ti, fc=fc, ps=s["t"]:
                                           e.transpose(ps[:, fc * 128:(fc + 1) * 128],
                                                       av_tok[:, ti * 512 + fc * 128: ti * 512 + (fc + 1) * 128],
                                                       id_bf),
                                           cT if fc == 3 else None)
                                P.v.wait(cT)
                                P.v.op(lambda e, ti=ti, ps=s["t"]:
                                       e.tensor_copy(avT3[:, 0:4, ti * 128:(ti + 1) * 128],
                                                     ps[:].rearrange("p (g t) -> p g t", g=4)), cV)
                                s["free"] = (cV, cV.v)
                            P.t.wait(dW, wo_w)
                            for mc in range(KC):
                                for tn in range(NTN):
                                    s = pW.acquire(P.t)
                                    for kc in range(4):
                                        P.t.op(lambda e, kc=kc, mc=mc, tn=tn, ps=s["t"]:
                                               e.matmul(ps[:],
                                                        wo_bf[:, kc * 1024 + mc * 128: kc * 1024 + (mc + 1) * 128],
                                                        avT[:, kc * S + tn * 512: kc * S + tn * 512 + 512],
                                                        start=(kc == 0), stop=(kc == 3)),
                                               cT if kc == 3 else None)
                                    P.s.wait(cT)
                                    P.s.op(lambda e, mc=mc, tn=tn, ps=s["t"]:
                                           e.add(ao[:, mc * S + tn * 512: mc * S + tn * 512 + 512],
                                                 ps[:], bo_sb[:, mc:mc + 1]), cS)
                                    s["free"] = (cS, cS.v)
                            ao_at = cS.v
                            dummy_g = [None, None]
                            for tn in range(NTN):
                                def src3(kc, tn=tn):
                                    a = ao[:, kc * S + tn * 512: kc * S + tn * 512 + 512]
                                    return a, [(cS, ao_at)], None
                                stats_at = ln_chunk(P, pSt3, stat3, src3, dummy_g)
                                P.v.wait(cV, stats_at)
                                for kc in range(KC):
                                    P.v.op(lambda e, kc=kc, tn=tn: e.tensor_tensor(
                                        out=stat3["t1"][:],
                                        in0=ao[:, kc * S + tn * 512: kc * S + tn * 512 + 512],
                                        in1=stat3["rinv"][:], op=ALU.mult))
                                    P.v.op(lambda e: e.tensor_tensor(
                                        out=stat3["var"][:], in0=stat3["t1"][:], in1=stat3["mrinv"][:],
                                        op=ALU.subtract))
                                    P.v.op(lambda e, kc=kc, tn=tn: e.tensor_tensor(
                                        out=resid[:, kc * S + tn * 512: kc * S + tn * 512 + 512],
                                        in0=resid[:, kc * S + tn * 512: kc * S + tn * 512 + 512],
                                        in1=stat3["var"][:], op=ALU.add), cV)
                            P.emit(block)
                            if b == 0:
                                dump_and_stop("b3", {"resid": resid[:], "ao": ao[:], "avT": avT[:]})

                ff = contextlib.ExitStack()
                with ff:
                    gu_ring = [sba(ff, f"gur{b}{i}", [128, 2048], BF16) for i in range(3)]
                    dn_ring = [sba(ff, f"dnr{b}0", [128, 6 * 1024], BF16), sba(ff, f"dnr{b}1", [128, 5 * 1024], BF16)]
                    gateT_bf = sba(ff, f"gateT{b}", [128, KC * NEXP], BF16)
                    seg_bf = sba(ff, f"seg{b}", [128, KC], BF16)
                    lnx = sba(ff, f"lnxf{b}", [128, KC * TH], BF16)
                    acc = sba(ff, f"acc{b}", [128, KC * TH], F32)
                    hbf = sba(ff, f"hbf{b}", [128, ICH * TH], BF16)
                    wbc = [sba(ff, f"wbc{b}{i}", [128, TH], BF16) for i in range(2)]
                    sigbc = sba(ff, f"sigbc{b}", [128, TH], BF16)
                    statf = mkstat(ff, f"f{b}", True)
                    sgt = [sba(ff, f"sgt{b}{i}", [128, 512], BF16) for i in range(2)]
                    uwt = [sba(ff, f"uwt{b}{i}", [128, 512], BF16) for i in range(2)]
                    g8 = {
                        "lg": sba(ff, f"lg{b}", [128, 8], F32),
                        "e8": sba(ff, f"e8{b}", [128, 8], F32),
                        "sc8": sba(ff, f"sc8{b}", [128, 8], F32),
                        "m": sba(ff, f"m8{b}", [128, 1], F32),
                        "ss": sba(ff, f"ss8{b}", [128, 1], F32),
                        "ri": sba(ff, f"ri8{b}", [128, 1], F32),
                        "msk": sba(ff, f"msk8{b}", [128, 8], F32),
                        "t8": sba(ff, f"t88{b}", [128, 8], F32),
                        "wst": sba(ff, f"wst{b}", [128, THC * 8], F32),
                        "wfb": sba(ff, f"wfb{b}", [8, TH], BF16),
                        "sgb": sba(ff, f"sgb{b}", [1, TH], BF16),
                    }
                    yst = [sba(ff, f"yst{b}{i}", [128, 512], F32) for i in range(2)]
                    pf = contextlib.ExitStack()
                    with pf:
                        pp = Pool([psa(pf, f"pp{b}{i}", [128, 512], F32) for i in range(8)])

                        P = new_phase()
                        P.g.op(lambda e: e.dma_start(out=gateT_bf[:], in_=ap["gate_i"]), dW, 16)
                        P.g.op(lambda e: e.dma_start(out=seg_bf[:], in_=ap["seg_i"]), dW, 16)
                        gate_w = dW.v

                        NGU = 3
                        gu_w = {}
                        gu_free = [None] * NGU
                        gu_last = {}
                        dn_w = {}
                        dn_free = [None] * 2

                        def gu_load(gidx):
                            p_, ic = divmod(gidx, ICH)
                            sl = gidx % NGU
                            if gu_free[sl] is not None:
                                P.g.wait(*gu_free[sl])
                            P.g.op(lambda e, p_=p_, ic=ic, sl=sl:
                                   e.dma_start(out=gu_ring[sl][:],
                                               in_=ap["wgu_i"][p_, :, ic * 2048:(ic + 1) * 2048]), dW, 16)
                            gu_w[gidx] = dW.v

                        def dn_load(didx):
                            p_, hk = divmod(didx, 2)
                            sl = didx % 2
                            ksz = 6 if hk == 0 else 5
                            if dn_free[sl] is not None:
                                P.g.wait(*dn_free[sl])
                            P.g.op(lambda e, p_=p_, hk=hk, sl=sl, ksz=ksz:
                                   e.dma_start(out=dn_ring[sl][:, 0:ksz * 1024],
                                               in_=ap["wdn_i"][p_, :, hk * 6 * 1024:(hk * 6 + ksz) * 1024]),
                                   dW, 16)
                            dn_w[didx] = dW.v

                        for tq in range(2):
                            t0 = tq * TH
                            for gidx in range(NGU):
                                gu_load(gidx)
                            for didx in range(2):
                                dn_load(didx)

                            xbfg = [None, None]
                            for tn in range(THN):
                                def srcf(kc, tn=tn):
                                    xb = statf["xbf"][kc % 2]
                                    if xbfg[kc % 2] is not None:
                                        P.v.wait(*xbfg[kc % 2])
                                    P.v.op(lambda e, kc=kc, xb=xb, tn=tn, t0=t0: e.tensor_copy(
                                        xb[:], resid[:, kc * S + t0 + tn * 512: kc * S + t0 + tn * 512 + 512]),
                                        cV)
                                    return xb[:], [(cV, cV.v)], kc % 2
                                stats_at = ln_chunk(P, pp, statf, srcf, xbfg)
                                P.v.wait(cV, stats_at)
                                for kc in range(KC):
                                    P.v.op(lambda e, kc=kc, tn=tn, t0=t0: e.tensor_tensor(
                                        out=statf["t1"][:],
                                        in0=resid[:, kc * S + t0 + tn * 512: kc * S + t0 + tn * 512 + 512],
                                        in1=statf["rinv"][:], op=ALU.mult))
                                    P.v.op(lambda e, kc=kc, tn=tn: e.tensor_tensor(
                                        out=lnx[:, kc * TH + tn * 512: kc * TH + tn * 512 + 512],
                                        in0=statf["t1"][:], in1=statf["mrinv"][:], op=ALU.subtract), cV)
                            lnx_at = cV.v

                            P.t.wait(dW, gate_w)
                            P.t.wait(cV, lnx_at)
                            for tc in range(THC):
                                s = pp.acquire(P.t)
                                for kc in range(KC):
                                    P.t.op(lambda e, kc=kc, tc=tc, ps=s["t"]:
                                           e.matmul(ps[:, 0:8],
                                                    lnx[:, kc * TH + tc * 128: kc * TH + (tc + 1) * 128],
                                                    gateT_bf[:, kc * 8:(kc + 1) * 8],
                                                    start=(kc == 0), stop=(kc == KC - 1)),
                                           cT if kc == KC - 1 else None)
                                P.v.wait(cT)
                                P.v.op(lambda e, ps=s["t"]: e.tensor_copy(g8["lg"][:], ps[:, 0:8]), cV)
                                s["free"] = (cV, cV.v)
                                P.v.wait(cV)
                                P.v.op(lambda e: e.reduce_max(g8["m"][:], g8["lg"][:], axis=AX, negate=True), cV)
                                P.s.wait(cV)
                                P.s.op(lambda e: e.activation(out=g8["e8"][:], in_=g8["lg"][:], func=AF.Exp,
                                                              bias=g8["m"][:], scale=1.0, accum_out=g8["ss"][:]),
                                       cS)
                                P.v.wait(cS)
                                P.v.op(lambda e: e.reciprocal(g8["ri"][:], g8["ss"][:]), cV)
                                P.v.wait(cV)
                                P.v.op(lambda e: e.tensor_scalar(out=g8["sc8"][:], in0=g8["e8"][:],
                                                                 scalar1=g8["ri"][:], scalar2=None, op0=ALU.mult), cV)
                                P.v.wait(cV)
                                P.v.op(lambda e: e.reduce_max(g8["m"][:], g8["sc8"][:], axis=AX, negate=True), cV)
                                P.v.wait(cV)
                                P.v.op(lambda e: e.tensor_scalar(out=g8["msk"][:], in0=g8["sc8"][:],
                                                                 scalar1=g8["m"][:], scalar2=0.0,
                                                                 op0=ALU.add, op1=ALU.is_ge), cV)
                                P.v.wait(cV)
                                P.v.op(lambda e: e.scalar_tensor_tensor(out=g8["t8"][:], in0=g8["msk"][:],
                                                                        scalar=-1e30, in1=g8["sc8"][:],
                                                                        op0=ALU.mult, op1=ALU.add), cV)
                                P.v.wait(cV)
                                P.v.op(lambda e: e.reduce_max(g8["m"][:], g8["t8"][:], axis=AX, negate=True), cV)
                                P.v.wait(cV)
                                P.v.op(lambda e: e.tensor_scalar(out=g8["msk"][:], in0=g8["sc8"][:],
                                                                 scalar1=g8["m"][:], scalar2=0.0,
                                                                 op0=ALU.add, op1=ALU.is_ge), cV)
                                P.v.wait(cV)
                                P.v.op(lambda e, tc=tc: e.tensor_tensor(out=g8["wst"][:, tc * 8:(tc + 1) * 8],
                                                                        in0=g8["sc8"][:], in1=g8["msk"][:],
                                                                        op=ALU.mult), cV)
                            wst_at = cV.v
                            for gtc in range(THC // 4):
                                s = pp.acquire(P.t)
                                P.t.wait(cV, wst_at)
                                for j in range(4):
                                    tc = gtc * 4 + j
                                    P.t.op(lambda e, tc=tc, j=j, ps=s["t"]:
                                           e.transpose(ps[0:8, j * 128:(j + 1) * 128],
                                                       g8["wst"][:, tc * 8:(tc + 1) * 8], cid_f[:]),
                                           cT if j == 3 else None)
                                P.v.wait(cT)
                                P.v.op(lambda e, gtc=gtc, ps=s["t"]:
                                       e.tensor_copy(g8["wfb"][0:8, gtc * 512:(gtc + 1) * 512], ps[0:8, 0:512]),
                                       cV)
                                s["free"] = (cV, cV.v)
                            wfb_at = cV.v

                            for tn in range(THN):
                                s = pp.acquire(P.t)
                                for kc in range(KC):
                                    P.t.op(lambda e, kc=kc, tn=tn, ps=s["t"]:
                                           e.matmul(ps[0:1, 0:512], seg_bf[:, kc:kc + 1],
                                                    lnx[:, kc * TH + tn * 512: kc * TH + tn * 512 + 512],
                                                    start=(kc == 0), stop=(kc == KC - 1)),
                                           cT if kc == KC - 1 else None)
                                P.s.wait(cT)
                                P.s.op(lambda e, tn=tn, ps=s["t"]:
                                       e.activation(out=g8["sgb"][0:1, tn * 512:(tn + 1) * 512],
                                                    in_=ps[0:1, 0:512], func=AF.Sigmoid), cS)
                                s["free"] = (cS, cS.v)
                            sgb_at = None
                            for tn in range(THN):
                                s = pp.acquire(P.t)
                                P.t.wait(cS)
                                P.t.op(lambda e, tn=tn, ps=s["t"]:
                                       e.matmul(ps[:], ones_bf[0:1, :], g8["sgb"][0:1, tn * 512:(tn + 1) * 512],
                                                start=True, stop=True), cT)
                                P.v.wait(cT)
                                P.v.op(lambda e, tn=tn, ps=s["t"]:
                                       e.tensor_copy(sigbc[:, tn * 512:(tn + 1) * 512], ps[:]), cV)
                                s["free"] = (cV, cV.v)
                            if b == 0 and tq == 0:
                                P.emit(block)
                                dump_and_stop("router", {"lnxf": lnx[:], "wst": g8["wst"][:],
                                                         "wf": g8["wfb"][:], "sigbc": sigbc[:],
                                                         "resid": resid[:],
                                                         "lg": g8["lg"][:], "sc8": g8["sc8"][:],
                                                         "msk": g8["msk"][:], "t8": g8["t8"][:]})
                                if stop in ("router",):
                                    raise _StopBuild()
                                P = new_phase()

                            hbf_guard = [None]
                            sgt_free = [None, None]
                            wbc_at = [0, 0]
                            for p_ in range(10):
                                if p_ < 8:
                                    for tn in range(THN):
                                        s = pp.acquire(P.t)
                                        P.t.wait(cV, wfb_at)
                                        P.t.op(lambda e, p_=p_, tn=tn, ps=s["t"]:
                                               e.matmul(ps[:], csel_sb[0:8, p_ * 128:(p_ + 1) * 128],
                                                        g8["wfb"][0:8, tn * 512:(tn + 1) * 512],
                                                        start=True, stop=True), cT)
                                        P.v.wait(cT)
                                        P.v.op(lambda e, p_=p_, tn=tn, ps=s["t"]:
                                               e.tensor_copy(wbc[p_ % 2][:, tn * 512:(tn + 1) * 512], ps[:]),
                                               cV)
                                        s["free"] = (cV, cV.v)
                                    wbc_at[p_ % 2] = cV.v
                                    wsel = wbc[p_ % 2]
                                    wsel_at = wbc_at[p_ % 2]
                                else:
                                    wsel = sigbc
                                    wsel_at = cV.v

                                for ic in range(ICH):
                                    gidx = (p_ * ICH + ic)
                                    sl = gidx % NGU
                                    P.t.wait(dW, gu_w[gidx])
                                    ps_g = [None, None]
                                    ps_u = [None, None]
                                    for gu in range(2):
                                        for tn in range(THN):
                                            s = pp.acquire(P.t)
                                            if hbf_guard[0] is not None and ic == 0 and gu == 0:
                                                pass
                                            for kc in range(KC):
                                                P.t.op(lambda e, gu=gu, kc=kc, tn=tn, sl=sl, ps=s["t"]:
                                                       e.matmul(ps[:],
                                                                gu_ring[sl][:, gu * 1024 + kc * 128: gu * 1024 + (kc + 1) * 128],
                                                                lnx[:, kc * TH + tn * 512: kc * TH + tn * 512 + 512],
                                                                start=(kc == 0), stop=(kc == KC - 1)),
                                                       cT if kc == KC - 1 else None)
                                            if gu == 0:
                                                ps_g[tn] = (s, cT.v)
                                            else:
                                                ps_u[tn] = (s, cT.v)
                                    gu_free[sl] = (cT, cT.v)
                                    if gidx + NGU < 10 * ICH:
                                        gu_load(gidx + NGU)
                                    silu_vs = [None] * THN
                                    for tn in range(THN):
                                        sG, gv = ps_g[tn]
                                        if sgt_free[tn] is not None:
                                            P.s.wait(*sgt_free[tn])
                                        P.s.wait(cT, gv)
                                        P.s.op(lambda e, tn=tn, ps=sG["t"]:
                                               e.activation(out=sgt[tn][:], in_=ps[:], func=AF.Silu), cS)
                                        sG["free"] = (cS, cS.v)
                                        silu_vs[tn] = cS.v
                                    for tn in range(THN):
                                        sU, uv = ps_u[tn]
                                        P.v.wait(cT, uv)
                                        P.v.wait(cV, wsel_at)
                                        P.v.op(lambda e, tn=tn, ps=sU["t"], wsel=wsel:
                                               e.tensor_tensor(out=uwt[tn][:], in0=ps[:],
                                                               in1=wsel[:, tn * 512:(tn + 1) * 512],
                                                               op=ALU.mult), cV)
                                        sU["free"] = (cV, cV.v)
                                    uw_done = cV.v
                                    P.v.wait(cV, uw_done) if THN == 1 else None
                                    for tn in range(THN):
                                        if hbf_guard[0] is not None:
                                            P.v.wait(*hbf_guard[0])
                                        P.v.wait(cS, silu_vs[tn])
                                        if THN == 1:
                                            P.v.wait(cV, uw_done)
                                        P.v.op(lambda e, ic=ic, tn=tn:
                                               e.tensor_tensor(out=hbf[:, ic * TH + tn * 512: ic * TH + tn * 512 + 512],
                                                               in0=sgt[tn][:], in1=uwt[tn][:], op=ALU.mult), cV)
                                        sgt_free[tn] = (cV, cV.v)
                                h_at = cV.v

                                for hk in range(2):
                                    didx = p_ * 2 + hk
                                    sl = didx % 2
                                    ksz = 6 if hk == 0 else 5
                                    kbase = hk * 6
                                    P.t.wait(dW, dn_w[didx])
                                    P.t.wait(cV, h_at)
                                    for hc in range(KC):
                                        for tn in range(THN):
                                            s = pp.acquire(P.t)
                                            for j in range(ksz):
                                                kc = kbase + j
                                                P.t.op(lambda e, j=j, kc=kc, hc=hc, tn=tn, sl=sl, ksz=ksz, ps=s["t"]:
                                                       e.matmul(ps[:],
                                                                dn_ring[sl][:, j * 1024 + hc * 128: j * 1024 + (hc + 1) * 128],
                                                                hbf[:, kc * TH + tn * 512: kc * TH + tn * 512 + 512],
                                                                start=(j == 0), stop=(j == ksz - 1)),
                                                       cT if j == ksz - 1 else None)
                                            P.v.wait(cT)
                                            if p_ == 0 and hk == 0:
                                                P.v.op(lambda e, hc=hc, tn=tn, ps=s["t"]:
                                                       e.tensor_copy(acc[:, hc * TH + tn * 512: hc * TH + tn * 512 + 512],
                                                                     ps[:]), cV)
                                            else:
                                                P.v.op(lambda e, hc=hc, tn=tn, ps=s["t"]:
                                                       e.scalar_tensor_tensor(
                                                           out=acc[:, hc * TH + tn * 512: hc * TH + tn * 512 + 512],
                                                           in0=ps[:], scalar=1.0,
                                                           in1=acc[:, hc * TH + tn * 512: hc * TH + tn * 512 + 512],
                                                           op0=ALU.mult, op1=ALU.add), cV)
                                            s["free"] = (cV, cV.v)
                                    dn_free[sl] = (cT, cT.v)
                                    if didx + 2 < 20:
                                        dn_load(didx + 2)
                                hbf_guard[0] = (cT, cT.v)
                                if b == 0 and tq == 0 and stop == f"exp{p_}":
                                    P.emit(block)
                                    dump_and_stop(f"exp{p_}", {"hbf": hbf[:], "acc": acc[:],
                                                               "wbc0": wbc[0][:], "lnxf": lnx[:],
                                                               "sgt0": sgt[0][:], "uwt0": uwt[0][:],
                                                               "cselt": csel_sb[:], "wfbt": g8["wfb"][:]})
                                    P = new_phase()
                            acc_at = cV.v
                            if b == 0 and tq == 0:
                                P.emit(block)
                                dump_and_stop("acc", {"acc": acc[:], "resid": resid[:]})
                                P = new_phase()

                            xbfg4 = [None, None]
                            for tn in range(THN):
                                def src4(kc, tn=tn):
                                    xb = statf["xbf"][kc % 2]
                                    if xbfg4[kc % 2] is not None:
                                        P.v.wait(*xbfg4[kc % 2])
                                    P.v.wait(cV, acc_at)
                                    P.v.op(lambda e, kc=kc, xb=xb, tn=tn: e.tensor_copy(
                                        xb[:], acc[:, kc * TH + tn * 512: kc * TH + tn * 512 + 512]), cV)
                                    return xb[:], [(cV, cV.v)], kc % 2
                                stats_at = ln_chunk(P, pp, statf, src4, xbfg4)
                                P.v.wait(cV, stats_at)
                                for kc in range(KC):
                                    P.v.op(lambda e, kc=kc, tn=tn: e.tensor_tensor(
                                        out=statf["t1"][:],
                                        in0=acc[:, kc * TH + tn * 512: kc * TH + tn * 512 + 512],
                                        in1=statf["rinv"][:], op=ALU.mult))
                                    P.v.op(lambda e: e.tensor_tensor(
                                        out=statf["var"][:], in0=statf["t1"][:], in1=statf["mrinv"][:],
                                        op=ALU.subtract))
                                    P.v.op(lambda e, kc=kc, tn=tn, t0=t0: e.tensor_tensor(
                                        out=resid[:, kc * S + t0 + tn * 512: kc * S + t0 + tn * 512 + 512],
                                        in0=resid[:, kc * S + t0 + tn * 512: kc * S + t0 + tn * 512 + 512],
                                        in1=statf["var"][:], op=ALU.add), cV)
                            resid_at = cV.v

                            yst_guard = [None, None]
                            for tc in range(THC):
                                tcg = tq * THC + tc
                                for g2 in range(2):
                                    s = pp.acquire(P.t)
                                    P.t.wait(cV, resid_at)
                                    for dc in range(4):
                                        dcg = g2 * 4 + dc
                                        P.t.op(lambda e, dcg=dcg, dc=dc, tcg=tcg, ps=s["t"]:
                                               e.transpose(ps[:, dc * 128:(dc + 1) * 128],
                                                           resid[:, dcg * S + tcg * 128: dcg * S + (tcg + 1) * 128],
                                                           cid_f[:]),
                                               cT if dc == 3 else None)
                                    P.v.wait(cT)
                                    if yst_guard[g2] is not None:
                                        P.v.wait(*yst_guard[g2])
                                    P.v.op(lambda e, g2=g2, ps=s["t"]:
                                           e.tensor_copy(yst[g2][:], ps[:]), cV)
                                    s["free"] = (cV, cV.v)
                                    P.y.wait(cV)
                                    P.y.op(lambda e, g2=g2, tcg=tcg:
                                           e.dma_start(out=y_out[b, tcg * 128:(tcg + 1) * 128,
                                                                 g2 * 512:(g2 + 1) * 512],
                                                       in_=yst[g2][:]), dOUT, 16)
                                    yst_guard[g2] = (dOUT, dOUT.v)
                        P.emit(block)

        phF = Phase()
        phF.y.wait(dOUT, dOUT.v)
        phF.emit(block)
        import os
        if os.environ.get("K_PRINT_SBUF"):
            print("min sbuf remaining (bytes/partition):", minrem[0])
    except _StopBuild:
        pass
    return nc


_CACHE = {}


def kernel(L0_attn_x, L0_attn_Wq, L0_attn_Wkv, L0_attn_Wo, L0_attn_bo, L0_attn_rel_emb,
           L0_ffn_gate_w, L0_ffn_experts_gate_up, L0_ffn_experts_down,
           L0_ffn_shared_gate_w, L0_ffn_shared_up_w, L0_ffn_shared_down_w,
           L0_ffn_shared_expert_gate_w):
    x = np.ascontiguousarray(np.asarray(L0_attn_x, dtype=np.float32))
    B, S, _ = x.shape
    BPC = B // NCORES

    lr_i, LRP = _build_lr(np.asarray(L0_attn_rel_emb, np.float32), S)
    wq_i = _img_lhsT(np.asarray(L0_attn_Wq, np.float32))
    wkv_i = _img_lhsT(np.asarray(L0_attn_Wkv, np.float32))
    wo_i = _img_lhsT(np.asarray(L0_attn_Wo, np.float32))
    bo_i = np.ascontiguousarray(np.asarray(L0_attn_bo, np.float32).reshape(8, 128).T)
    gate_i = _img_lhsT(np.ascontiguousarray(np.asarray(L0_ffn_gate_w, np.float32).T))
    seg_i = _img_lhsT(np.ascontiguousarray(np.asarray(L0_ffn_shared_expert_gate_w, np.float32).T))

    egu = np.asarray(L0_ffn_experts_gate_up, np.float32)
    edn = np.asarray(L0_ffn_experts_down, np.float32)
    sg = np.asarray(L0_ffn_shared_gate_w, np.float32)
    su = np.asarray(L0_ffn_shared_up_w, np.float32)
    sd = np.asarray(L0_ffn_shared_down_w, np.float32)

    IEXP_ = ICH * 128
    wgu_list = [_img_gu(egu[e, :IEXP_], egu[e, IEXP_:]) for e in range(NEXP)]
    wgu_list.append(_img_gu(sg[:IEXP_], su[:IEXP_]))
    wgu_list.append(_img_gu(sg[IEXP_:], su[IEXP_:]))
    wgu_i = np.ascontiguousarray(np.stack(wgu_list, axis=0))
    wdn_list = [_img_down(edn[e]) for e in range(NEXP)]
    wdn_list.append(_img_down(sd[:, :IEXP_]))
    wdn_list.append(_img_down(sd[:, IEXP_:]))
    wdn_i = np.ascontiguousarray(np.stack(wdn_list, axis=0))

    cid_f = np.eye(128, dtype=np.float32)
    ceps = np.full((128, 1), 1e-5, np.float32)
    csel = np.zeros((8, 8 * 128), np.float32)
    for e in range(8):
        csel[e, e * 128:(e + 1) * 128] = 1.0
    csel = csel.astype(BF)
    cmisc = np.concatenate([np.eye(128, dtype=np.float32),
                            np.eye(128, dtype=np.float32) * 8.0,
                            np.ones((128, 128), np.float32)], axis=1).astype(BF)

    key = (S, BPC)
    if key not in _CACHE:
        _CACHE[key] = build_program(S, BPC, LRP)
    nc = _CACHE[key]

    common = dict(wq_i=wq_i, wkv_i=wkv_i, wo_i=wo_i, bo_i=bo_i, gate_i=gate_i, seg_i=seg_i,
                  lr_i=lr_i, wgu_i=wgu_i, wdn_i=wdn_i, cid_f=cid_f, cmisc=cmisc, ceps=ceps, csel=csel)
    in_maps = [dict(common, x=np.ascontiguousarray(x[c * BPC:(c + 1) * BPC]))
               for c in range(NCORES)]
    res = run_bass_kernel_spmd(nc, in_maps, core_ids=list(range(NCORES)))
    return np.concatenate([r["y"] for r in res.results], axis=0)


# version 04
# speedup vs baseline: 1.0227x; 1.0227x over previous
import contextlib
import math

import numpy as np
import ml_dtypes

import concourse.bass as bass
import concourse.mybir as mybir
from concourse.bass_utils import run_bass_kernel_spmd

F32 = mybir.dt.float32
BF16 = mybir.dt.bfloat16
AF = mybir.ActivationFunctionType
ALU = mybir.AluOpType
AX = mybir.AxisListType.X

NCORES = 8
D = 1024
KC = 8
HEADS = 8
DH = 64
INNER = 512
NEXP = 8
ICH = 11
MASKNEG = -1.25e29

BF = ml_dtypes.bfloat16


def _bucket_of_d(S):
    d = np.arange(S)
    n = np.maximum(d, 0)
    lg = np.log(np.maximum(n, 1).astype(np.float32) / np.float32(16.0))
    val = np.int32(16) + (lg / np.float32(math.log(8.0)) * np.float32(16.0)).astype(np.int32)
    val = np.minimum(val, 31)
    return np.where(n < 16, n, val).astype(np.int64)


def _build_lr(rel_emb, S):
    P0 = S + 127
    lpad = np.zeros((HEADS, S + 254), np.float32)
    lpad[:, :127] = MASKNEG
    buck = _bucket_of_d(S)
    lpad[:, 127:127 + S] = rel_emb.astype(np.float32).T[:, buck]
    v = np.arange(P0)
    p = np.arange(128)
    idx = (P0 - 1) - v[None, :] + p[:, None]
    lr = lpad[:, idx]
    pad = (-P0) % 16
    if pad:
        lr = np.concatenate([lr, np.zeros((HEADS, 128, pad), np.float32)], axis=2)
    return np.ascontiguousarray(lr), P0 + pad


def _img_lhsT(w):
    K, Mt = w.shape
    kc = K // 128
    return np.ascontiguousarray(
        w.reshape(kc, 128, Mt).transpose(1, 0, 2).reshape(128, kc * Mt).astype(np.float32))


def _img_gu(g_rows, u_rows):
    out = np.empty((128, ICH * 2048), np.float32)
    for gu, W in enumerate((g_rows, u_rows)):
        a = W.T.reshape(KC, 128, ICH, 128).transpose(1, 2, 0, 3)
        for ic in range(ICH):
            base = ic * 2048 + gu * 1024
            out[:, base:base + 1024] = a[:, ic].reshape(128, 1024)
    return np.ascontiguousarray(out)


def _img_down(dn):
    a = dn.T.reshape(ICH, 128, KC, 128).transpose(1, 0, 2, 3)
    return np.ascontiguousarray(a.reshape(128, ICH * 1024).astype(np.float32))


class Ctr:
    def __init__(self, es, nc, name):
        self.sem = es.enter_context(nc.semaphore(name))
        self.name = name
        self.v = 0


class Stream:
    def __init__(self, name):
        self.name = name
        self.ops = []
        self.last_wait = {}

    def wait(self, ctr, v=None):
        if v is None:
            v = ctr.v
        if v <= 0 or self.last_wait.get(ctr.name, -1) >= v:
            return
        self.last_wait[ctr.name] = v
        self.ops.append(lambda e, s=ctr.sem, vv=v: e.wait_ge(s, vv))

    def op(self, fn, inc=None, n=1):
        if inc is not None:
            inc.v += n
            self.ops.append(lambda e, f=fn, s=inc.sem, nn=n: f(e).then_inc(s, nn))
            return inc.v
        self.ops.append(fn)
        return None


class Phase:
    def __init__(self):
        self.t = Stream("t")
        self.v = Stream("v")
        self.s = Stream("s")
        self.y = Stream("y")
        self.g = Stream("g")

    def emit(self, block):
        for st, reg in ((self.g, block.gpsimd), (self.y, block.sync), (self.t, block.tensor),
                        (self.v, block.vector), (self.s, block.scalar)):
            if st.ops:
                ops = st.ops
                reg(lambda e, ops=ops: [f(e) for f in ops])


class Pool:
    def __init__(self, slots):
        self.slots = [{"t": s, "free": None} for s in slots]
        self.i = 0

    def acquire(self, stream):
        s = self.slots[self.i % len(self.slots)]
        self.i += 1
        if s["free"] is not None:
            stream.wait(*s["free"])
        return s


class _StopBuild(Exception):
    pass


def build_program(S, BPC, LRP, stop=None):
    nc = bass.Bass("TRN2", target_bir_lowering=False)
    NT = S // 128
    NTN = S // 512
    TH = S // 2
    THN = TH // 512
    THC = TH // 128

    ap = {}
    def din(name, shape, dt=F32):
        ap[name] = nc.dram_tensor(name, shape, dt, kind="ExternalInput").ap()
    din("x", [BPC, S, D])
    din("wq_i", [128, KC * INNER])
    din("wkv_i", [128, 8 * 1024])
    din("wo_i", [128, 4 * 1024])
    din("bo_i", [128, 8])
    din("gate_i", [128, KC * NEXP])
    din("seg_i", [128, KC])
    din("lr_i", [HEADS, 128, LRP])
    din("wgu_i", [10, 128, ICH * 2048])
    din("wdn_i", [10, 128, ICH * 1024])
    din("cid_f", [128, 128])
    din("ceps", [128, 1])
    din("csel", [8, 8 * 128], BF16)
    din("cmisc", [128, 3 * 128], BF16)
    y_out = nc.dram_tensor("y", [BPC, S, D], F32, kind="ExternalOutput").ap()

    es = contextlib.ExitStack()
    try:
      with es:
        cT = Ctr(es, nc, "cT")
        cV = Ctr(es, nc, "cV")
        cS = Ctr(es, nc, "cS")
        dW = Ctr(es, nc, "dW")
        dIO = Ctr(es, nc, "dIO")
        dOUT = Ctr(es, nc, "dOUT")
        ALLC = [cT, cV, cS, dW, dIO, dOUT]

        minrem = [10 ** 9]

        def sba(stack, n, sh, dt):
            t = stack.enter_context(nc.sbuf_tensor(n, sh, dt))
            minrem[0] = min(minrem[0], nc.sbuf_bytes_remaining)
            return t

        def psa(stack, n, sh, dt=F32):
            return stack.enter_context(nc.psum_tensor(n, sh, dt))

        cid_f = sba(es, "cid_f_sb", [128, 128], F32)
        cmisc = sba(es, "cmisc_sb", [128, 3 * 128], BF16)
        id_bf = cmisc[:, 0:128]
        eye8_bf = cmisc[:, 128:256]
        ones_bf = cmisc[:, 256:384]
        bo_sb = sba(es, "bo_sb0", [128, 8], F32)
        ceps_sb = sba(es, "ceps_sb", [128, 1], F32)
        csel_sb = sba(es, "csel_sb", [8, 8 * 128], BF16)
        nc.const_aps.register(F32, 1e-5, ceps_sb[:, 0:1])

        block = es.enter_context(nc.Block())

        def new_phase():
            P = Phase()
            for st in (P.t, P.v, P.s, P.y, P.g):
                for c in ALLC:
                    st.wait(c, c.v)
            return P

        def dump_and_stop(label, bufs):
            if stop != label:
                return
            Pd = Phase()
            for c in ALLC:
                Pd.y.wait(c, c.v)
            for name, a in bufs.items():
                t = nc.dram_tensor("tap_" + name, list(a.shape), a.dtype,
                                   kind="ExternalOutput").ap()
                Pd.y.op(lambda e, t=t, a=a: e.dma_start(out=t, in_=a), dOUT, 16)
            Pd.y.wait(dOUT, dOUT.v)
            Pd.emit(block)
            raise _StopBuild()

        ph = Phase()
        ph.g.op(lambda e: e.dma_start(out=cmisc[:], in_=ap["cmisc"]), dW, 16)
        ph.y.op(lambda e: e.dma_start(out=cid_f[:], in_=ap["cid_f"]), dIO, 16)
        ph.y.op(lambda e: e.dma_start(out=bo_sb[:], in_=ap["bo_i"]), dIO, 16)
        ph.y.op(lambda e: e.dma_start(out=ceps_sb[:], in_=ap["ceps"]), dIO, 16)
        ph.g.op(lambda e: e.dma_start(out=csel_sb[:], in_=ap["csel"]), dW, 16)
        ph.emit(block)

        def ln_chunk(P, pool, stat, src, guards):
            s1 = pool.acquire(P.t)
            s2 = pool.acquire(P.t)
            for kc in range(KC):
                a, ws, gslot = src(kc)
                for w in ws:
                    P.s.wait(*w)
                sq = stat["sq"][kc % 2]
                if stat["sq_free"][kc % 2] is not None:
                    P.s.wait(*stat["sq_free"][kc % 2])
                P.s.op(lambda e, a=a, q=sq: e.activation(out=q[:], in_=a, func=AF.Square), cS)
                P.t.wait(cS)
                for w in ws:
                    P.t.wait(*w)
                P.t.op(lambda e, a=a, st=(kc == 0), sp=(kc == KC - 1), ps=s1["t"]:
                       e.matmul(ps[:, 0:512], ones_bf, a, start=st, stop=sp), cT)
                if gslot is not None:
                    guards[gslot] = (cT, cT.v)
                P.t.op(lambda e, q=sq, st=(kc == 0), sp=(kc == KC - 1), ps=s2["t"]:
                       e.matmul(ps[:, 0:512], ones_bf, q[:], start=st, stop=sp), cT)
                stat["sq_free"][kc % 2] = (cT, cT.v)
            P.v.wait(cT)
            P.v.op(lambda e, ps=s1["t"]: e.tensor_scalar(out=stat["mean"][:], in0=ps[:, 0:512],
                                                         scalar1=1.0 / D, scalar2=None, op0=ALU.mult))
            P.v.op(lambda e: e.tensor_tensor(out=stat["var"][:], in0=stat["mean"][:],
                                             in1=stat["mean"][:], op=ALU.mult))
            P.v.op(lambda e, ps=s2["t"]: e.scalar_tensor_tensor(out=stat["var"][:], in0=ps[:, 0:512],
                                                                scalar=1.0 / D, in1=stat["var"][:],
                                                                op0=ALU.mult, op1=ALU.subtract), cV)
            s1["free"] = (cV, cV.v)
            s2["free"] = (cV, cV.v)
            P.s.wait(cV)
            P.s.op(lambda e: e.activation(out=stat["var"][:], in_=stat["var"][:], func=AF.Sqrt,
                                          bias=1e-5, scale=1.0), cS)
            P.v.wait(cS)
            P.v.op(lambda e: e.reciprocal(stat["rinv"][:], stat["var"][:]), cV)
            P.v.wait(cV)
            P.v.op(lambda e: e.tensor_tensor(out=stat["mrinv"][:], in0=stat["mean"][:],
                                             in1=stat["rinv"][:], op=ALU.mult), cV)
            P.v.wait(cV)
            return cV.v

        def mkstat(stack, tag, with_xbf):
            st = {
                "mean": sba(stack, f"mean{tag}", [128, 512], F32),
                "var": sba(stack, f"var{tag}", [128, 512], F32),
                "rinv": sba(stack, f"rinv{tag}", [128, 512], F32),
                "mrinv": sba(stack, f"mrinv{tag}", [128, 512], F32),
                "sq": [sba(stack, f"sq{tag}{i}", [128, 512], BF16) for i in range(2)],
                "sq_free": [None, None],
                "t1": sba(stack, f"t1{tag}", [128, 512], F32),
            }
            if with_xbf:
                st["xbf"] = [sba(stack, f"xbf{tag}{i}", [128, 512], BF16) for i in range(2)]
            return st

        for b in range(BPC):
            esb = contextlib.ExitStack()
            with esb:
                resid = sba(esb, f"resid{b}", [128, KC * S], F32)
                resid3 = resid[:].rearrange("p (kc t) -> p kc t", kc=KC)

                kqv = contextlib.ExitStack()
                with kqv:
                    qT = sba(kqv, f"qT{b}", [128, 4 * S], BF16)
                    kT = sba(kqv, f"kT{b}", [128, 4 * S], BF16)
                    vtok = sba(kqv, f"vtok{b}", [128, NT * 512], BF16)
                    av_tok = sba(kqv, f"avtok{b}", [128, NT * 512], BF16)

                    b1 = contextlib.ExitStack()
                    with b1:
                        wq_bf = sba(b1, f"wq{b}", [128, KC * INNER], BF16)
                        wkv_bf = sba(b1, f"wkv{b}", [128, 8 * 1024], BF16)
                        lnx2 = [sba(b1, f"lnx{b}{i}", [128, KC * 512], BF16) for i in range(2)]
                        xin = [sba(b1, f"xin{b}{i}", [128, D], F32) for i in range(2)]
                        stat = mkstat(b1, f"a{b}", True)
                        pb1 = contextlib.ExitStack()
                        with pb1:
                            pX = Pool([psa(pb1, f"pX{b}{i}", [128, 512], F32) for i in range(2)])
                            pSt = Pool([psa(pb1, f"pSt{b}{i}", [128, 512], F32) for i in range(2)])
                            pQ = Pool([psa(pb1, f"pQ{b}{i}", [128, 512], F32) for i in range(4)])

                            P = new_phase()
                            P.g.op(lambda e: e.dma_start(out=wq_bf[:], in_=ap["wq_i"]), dW, 16)
                            P.g.op(lambda e: e.dma_start(out=wkv_bf[:], in_=ap["wkv_i"]), dW, 16)
                            wqkv_w = dW.v

                            xin_free = [None, None]
                            for tc in range(NT):
                                xi = xin[tc % 2]
                                if xin_free[tc % 2] is not None:
                                    P.y.wait(*xin_free[tc % 2])
                                P.y.op(lambda e, xi=xi, tc=tc:
                                       e.dma_start(out=xi[:], in_=ap["x"][b, tc * 128:(tc + 1) * 128, :]),
                                       dIO, 16)
                                P.t.wait(dIO)
                                for g2 in range(2):
                                    s = pX.acquire(P.t)
                                    for dc in range(4):
                                        dcg = g2 * 4 + dc
                                        P.t.op(lambda e, xi=xi, dcg=dcg, dc=dc, ps=s["t"]:
                                               e.transpose(ps[:, dc * 128:(dc + 1) * 128],
                                                           xi[:, dcg * 128:(dcg + 1) * 128], cid_f[:]),
                                               cT if dc == 3 else None)
                                    xin_free[tc % 2] = (cT, cT.v)
                                    P.v.wait(cT)
                                    P.v.op(lambda e, tc=tc, g2=g2, ps=s["t"]:
                                           e.tensor_copy(resid3[:, g2 * 4:(g2 + 1) * 4, tc * 128:(tc + 1) * 128],
                                                         ps[:].rearrange("p (g t) -> p g t", g=4)), cV)
                                    s["free"] = (cV, cV.v)
                            resid_ready = cV.v

                            lnx_free = [None, None]
                            xbf_guards = [None, None]
                            P.t.wait(dW, wqkv_w)
                            for tn in range(NTN):
                                def src1(kc, tn=tn):
                                    xb = stat["xbf"][kc % 2]
                                    if xbf_guards[kc % 2] is not None:
                                        P.v.wait(*xbf_guards[kc % 2])
                                    P.v.wait(cV, resid_ready)
                                    P.v.op(lambda e, kc=kc, xb=xb, tn=tn: e.tensor_copy(
                                        xb[:], resid[:, kc * S + tn * 512: kc * S + tn * 512 + 512]), cV)
                                    return xb[:], [(cV, cV.v)], kc % 2
                                stats_at = ln_chunk(P, pSt, stat, src1, xbf_guards)

                                lnx = lnx2[tn % 2]
                                P.v.wait(cV, stats_at)
                                if lnx_free[tn % 2] is not None:
                                    P.v.wait(*lnx_free[tn % 2])
                                for kc in range(KC):
                                    P.v.op(lambda e, kc=kc, tn=tn: e.tensor_tensor(
                                        out=stat["t1"][:],
                                        in0=resid[:, kc * S + tn * 512: kc * S + tn * 512 + 512],
                                        in1=stat["rinv"][:], op=ALU.mult))
                                    P.v.op(lambda e, kc=kc, lnx=lnx: e.tensor_tensor(
                                        out=lnx[:, kc * 512:(kc + 1) * 512], in0=stat["t1"][:],
                                        in1=stat["mrinv"][:], op=ALU.subtract), cV)
                                lnx_at = cV.v

                                for qk in range(2):
                                    for mc in range(4):
                                        s = pQ.acquire(P.t)
                                        P.t.wait(cV, lnx_at)
                                        for kc in range(KC):
                                            if qk == 0:
                                                lhs = wq_bf[:, kc * 512 + mc * 128: kc * 512 + (mc + 1) * 128]
                                            else:
                                                lhs = wkv_bf[:, kc * 1024 + mc * 128: kc * 1024 + (mc + 1) * 128]
                                            P.t.op(lambda e, lhs=lhs, kc=kc, lnx=lnx, ps=s["t"]:
                                                   e.matmul(ps[:], lhs, lnx[:, kc * 512:(kc + 1) * 512],
                                                            start=(kc == 0), stop=(kc == KC - 1)),
                                                   cT if kc == KC - 1 else None)
                                        P.s.wait(cT)
                                        if qk == 0:
                                            P.s.op(lambda e, mc=mc, tn=tn, ps=s["t"]:
                                                   e.mul(qT[:, mc * S + tn * 512: mc * S + tn * 512 + 512],
                                                         ps[:], 0.125), cS)
                                        else:
                                            P.s.op(lambda e, mc=mc, tn=tn, ps=s["t"]:
                                                   e.copy(kT[:, mc * S + tn * 512: mc * S + tn * 512 + 512],
                                                          ps[:]), cS)
                                        s["free"] = (cS, cS.v)
                                for tcl in range(4):
                                    s = pQ.acquire(P.t)
                                    for kc in range(KC):
                                        P.t.op(lambda e, kc=kc, tcl=tcl, lnx=lnx, ps=s["t"]:
                                               e.matmul(ps[:],
                                                        lnx[:, kc * 512 + tcl * 128: kc * 512 + (tcl + 1) * 128],
                                                        wkv_bf[:, kc * 1024 + 512: kc * 1024 + 1024],
                                                        start=(kc == 0), stop=(kc == KC - 1)),
                                               cT if kc == KC - 1 else None)
                                    lnx_free[tn % 2] = (cT, cT.v)
                                    P.s.wait(cT)
                                    tcg = tn * 4 + tcl
                                    P.s.op(lambda e, tcg=tcg, ps=s["t"]:
                                           e.copy(vtok[:, tcg * 512:(tcg + 1) * 512], ps[:]), cS)
                                    s["free"] = (cS, cS.v)
                            P.emit(block)
                            if b == 0:
                                dump_and_stop("b1", {"resid": resid[:], "qT": qT[:], "kT": kT[:],
                                                     "vtok": vtok[:], "lnx0": lnx2[0][:], "lnx1": lnx2[1][:]})

                    b2 = contextlib.ExitStack()
                    with b2:
                        lr_bf = [sba(b2, f"lr{b}{i}", [128, LRP], BF16) for i in range(2)]
                        aexp = [sba(b2, f"aexp{b}{i}", [128, S], BF16) for i in range(2)]
                        attnT = sba(b2, f"attnT{b}", [128, S], BF16)
                        nm4 = sba(b2, f"nm4{b}", [128, 4], F32)
                        nmax = sba(b2, f"nmax{b}", [128, 1], F32)
                        sums = sba(b2, f"sums{b}", [128, 1], F32)
                        rnv = [sba(b2, f"rnv{b}{i}", [128, 1], F32) for i in range(2)]
                        pb2 = contextlib.ExitStack()
                        with pb2:
                            pSc = psa(pb2, f"pSc{b}", [128, min(S, 2048)], F32)
                            pTr = Pool([psa(pb2, f"pTr{b}{i}", [128, 1024], BF16) for i in range(2)])
                            pAV = Pool([psa(pb2, f"pAV{b}{i}", [128, 64], F32) for i in range(2)])

                            P = new_phase()
                            lr_w = [None, None]
                            lr_free = [None, None]
                            exp_done = [0]
                            aexp_free = [None, None]
                            pend = [None]

                            def flush(P=P):
                                fh, fti, fexp_at, fpar = pend[0]
                                ng = (fti + 1 + 7) // 8
                                for g in range(ng):
                                    j0 = g * 8
                                    j1 = min(j0 + 8, fti + 1)
                                    s = pTr.acquire(P.t)
                                    P.t.wait(cS, fexp_at)
                                    for jc in range(j0, j1):
                                        P.t.op(lambda e, jc=jc, j0=j0, fpar=fpar, ps=s["t"]:
                                               e.transpose(ps[:, (jc - j0) * 128:(jc - j0 + 1) * 128],
                                                           aexp[fpar][:, jc * 128:(jc + 1) * 128], id_bf),
                                               cT if jc == j1 - 1 else None)
                                    if g % 2 == 0:
                                        P.s.wait(cT)
                                        P.s.op(lambda e, j0=j0, j1=j1, ps=s["t"]:
                                               e.copy(attnT[:, j0 * 128: j1 * 128],
                                                      ps[:, 0:(j1 - j0) * 128]), cS)
                                        s["free"] = (cS, cS.v)
                                        at_s = cS.v
                                    else:
                                        P.v.wait(cT)
                                        P.v.op(lambda e, j0=j0, j1=j1, ps=s["t"]:
                                               e.tensor_copy(attnT[:, j0 * 128: j1 * 128],
                                                             ps[:, 0:(j1 - j0) * 128]), cV)
                                        s["free"] = (cV, cV.v)
                                aexp_free[fpar] = (cT, cT.v)
                                sav = pAV.acquire(P.t)
                                P.t.wait(cS)
                                P.t.wait(cV)
                                for jc in range(fti + 1):
                                    P.t.op(lambda e, jc=jc, fh=fh, ps=sav["t"]:
                                           e.matmul(ps[:],
                                                    attnT[:, jc * 128:(jc + 1) * 128],
                                                    vtok[:, jc * 512 + fh * 64: jc * 512 + fh * 64 + 64],
                                                    start=(jc == 0), stop=(jc == fti)),
                                           cT if jc == fti else None)
                                P.s.wait(cT)
                                P.s.op(lambda e, fh=fh, fti=fti, fpar=fpar, ps=sav["t"]:
                                       e.mul(av_tok[:, fti * 512 + fh * 64: fti * 512 + fh * 64 + 64],
                                             ps[:], rnv[fpar][:]), cS)
                                sav["free"] = (cS, cS.v)
                                pend[0] = None

                            for h in range(HEADS):
                                if lr_free[h % 2] is not None:
                                    P.g.wait(*lr_free[h % 2])
                                P.g.op(lambda e, h=h: e.dma_start(out=lr_bf[h % 2][:], in_=ap["lr_i"][h]), dW, 16)
                                lr_w[h % 2] = dW.v
                                qrow = (h % 2) * 64
                                qcol = (h // 2) * S
                                for ti in range(NT):
                                    Nj = (ti + 1) * 128
                                    P.t.wait(dW, lr_w[h % 2])
                                    P.t.wait(cS, exp_done[0])
                                    v0 = (S - 1) - ti * 128
                                    nch = (Nj + 511) // 512
                                    chunk_t = []
                                    for c in range(nch):
                                        cw = min(512, Nj - c * 512)
                                        P.t.op(lambda e, c=c, cw=cw, v0=v0, par=h % 2:
                                               e.matmul(pSc[:, c * 512: c * 512 + cw], eye8_bf,
                                                        lr_bf[par][:, v0 + c * 512: v0 + c * 512 + cw],
                                                        start=True, stop=False))
                                        P.t.op(lambda e, ti=ti, c=c, cw=cw, qrow=qrow, qcol=qcol:
                                               e.matmul(pSc[:, c * 512: c * 512 + cw],
                                                        qT[qrow:qrow + 64, qcol + ti * 128: qcol + (ti + 1) * 128],
                                                        kT[qrow:qrow + 64, qcol + c * 512: qcol + c * 512 + cw],
                                                        start=False, stop=True), cT)
                                        chunk_t.append(cT.v)
                                    scores_t = cT.v
                                    P.v.wait(cS, exp_done[0])
                                    for c in range(nch):
                                        cw = min(512, Nj - c * 512)
                                        P.v.wait(cT, chunk_t[c])
                                        P.v.op(lambda e, c=c, cw=cw:
                                               e.reduce_max(nm4[:, c:c + 1], pSc[:, c * 512: c * 512 + cw],
                                                            axis=AX, negate=True),
                                               cV if c == nch - 1 else None)
                                    P.v.wait(cV)
                                    if nch > 1:
                                        P.v.op(lambda e, nch=nch:
                                               e.tensor_reduce(nmax[:], nm4[:, 0:nch], axis=AX, op=ALU.min),
                                               cV)
                                        P.v.wait(cV)
                                        bias_ap = nmax
                                    else:
                                        bias_ap = None
                                    max_v = cV.v
                                    if aexp_free[ti % 2] is not None:
                                        P.s.wait(*aexp_free[ti % 2])
                                    P.s.wait(cV, max_v)
                                    P.s.op(lambda e, Nj=Nj, par=ti % 2, ba=bias_ap:
                                           e.activation(out=aexp[par][:, 0:Nj], in_=pSc[:, 0:Nj], func=AF.Exp,
                                                        bias=(ba[:] if ba is not None else nm4[:, 0:1]),
                                                        scale=1.0, accum_out=sums[:]), cS)
                                    exp_done[0] = cS.v
                                    P.v.wait(cS, exp_done[0])
                                    P.v.op(lambda e, par=ti % 2: e.reciprocal(rnv[par][:], sums[:]), cV)
                                    P.v.wait(cV)
                                    exp_at = exp_done[0]
                                    if pend[0] is not None:
                                        flush()
                                    pend[0] = (h, ti, exp_at, ti % 2)
                                    if b == 0 and h == 0 and ti == 1 and stop == "b2s":
                                        flush()
                                        P.emit(block)
                                        dump_and_stop("b2s", {
                                            "aexp0": aexp[0][:],
                                            "aexp1": aexp[1][:],
                                            "attnT": attnT[:], "avtok": av_tok[:],
                                            "nmax": nmax[:], "sums": sums[:]})
                                lr_free[h % 2] = (cT, cT.v)
                            flush()
                            lr_free[(HEADS - 1) % 2] = (cT, cT.v)
                            P.emit(block)
                            if b == 0:
                                dump_and_stop("b2", {"avtok": av_tok[:], "attnT": attnT[:],
                                                     "aexp0": aexp[0][:],
                                                     "qT": qT[:], "kT": kT[:], "vtok": vtok[:]})

                    b3 = contextlib.ExitStack()
                    with b3:
                        wo_bf = sba(b3, f"wo{b}", [128, 4 * 1024], BF16)
                        avT = sba(b3, f"avT{b}", [128, 4 * S], BF16)
                        ao = sba(b3, f"ao{b}", [128, KC * S], BF16)
                        stat3 = mkstat(b3, f"c{b}", False)
                        pb3 = contextlib.ExitStack()
                        with pb3:
                            pW = Pool([psa(pb3, f"pW{b}{i}", [128, 512], F32) for i in range(2)])
                            pSt3 = Pool([psa(pb3, f"pSt3{b}{i}", [128, 512], F32) for i in range(2)])
                            pTrB = Pool([psa(pb3, f"pTrB{b}{i}", [128, 512], BF16) for i in range(2)])
                            P = new_phase()
                            P.g.op(lambda e: e.dma_start(out=wo_bf[:], in_=ap["wo_i"]), dW, 16)
                            wo_w = dW.v
                            avT3 = avT[:].rearrange("p (kc t) -> p kc t", kc=4)
                            for ti in range(NT):
                                s = pTrB.acquire(P.t)
                                for fc in range(4):
                                    P.t.op(lambda e, ti=ti, fc=fc, ps=s["t"]:
                                           e.transpose(ps[:, fc * 128:(fc + 1) * 128],
                                                       av_tok[:, ti * 512 + fc * 128: ti * 512 + (fc + 1) * 128],
                                                       id_bf),
                                           cT if fc == 3 else None)
                                P.v.wait(cT)
                                P.v.op(lambda e, ti=ti, ps=s["t"]:
                                       e.tensor_copy(avT3[:, 0:4, ti * 128:(ti + 1) * 128],
                                                     ps[:].rearrange("p (g t) -> p g t", g=4)), cV)
                                s["free"] = (cV, cV.v)
                            P.t.wait(dW, wo_w)
                            for mc in range(KC):
                                for tn in range(NTN):
                                    s = pW.acquire(P.t)
                                    for kc in range(4):
                                        P.t.op(lambda e, kc=kc, mc=mc, tn=tn, ps=s["t"]:
                                               e.matmul(ps[:],
                                                        wo_bf[:, kc * 1024 + mc * 128: kc * 1024 + (mc + 1) * 128],
                                                        avT[:, kc * S + tn * 512: kc * S + tn * 512 + 512],
                                                        start=(kc == 0), stop=(kc == 3)),
                                               cT if kc == 3 else None)
                                    P.s.wait(cT)
                                    P.s.op(lambda e, mc=mc, tn=tn, ps=s["t"]:
                                           e.add(ao[:, mc * S + tn * 512: mc * S + tn * 512 + 512],
                                                 ps[:], bo_sb[:, mc:mc + 1]), cS)
                                    s["free"] = (cS, cS.v)
                            ao_at = cS.v
                            dummy_g = [None, None]
                            for tn in range(NTN):
                                def src3(kc, tn=tn):
                                    a = ao[:, kc * S + tn * 512: kc * S + tn * 512 + 512]
                                    return a, [(cS, ao_at)], None
                                stats_at = ln_chunk(P, pSt3, stat3, src3, dummy_g)
                                P.v.wait(cV, stats_at)
                                for kc in range(KC):
                                    P.v.op(lambda e, kc=kc, tn=tn: e.tensor_tensor(
                                        out=stat3["t1"][:],
                                        in0=ao[:, kc * S + tn * 512: kc * S + tn * 512 + 512],
                                        in1=stat3["rinv"][:], op=ALU.mult))
                                    P.v.op(lambda e: e.tensor_tensor(
                                        out=stat3["var"][:], in0=stat3["t1"][:], in1=stat3["mrinv"][:],
                                        op=ALU.subtract))
                                    P.v.op(lambda e, kc=kc, tn=tn: e.tensor_tensor(
                                        out=resid[:, kc * S + tn * 512: kc * S + tn * 512 + 512],
                                        in0=resid[:, kc * S + tn * 512: kc * S + tn * 512 + 512],
                                        in1=stat3["var"][:], op=ALU.add), cV)
                            P.emit(block)
                            if b == 0:
                                dump_and_stop("b3", {"resid": resid[:], "ao": ao[:], "avT": avT[:]})

                ff = contextlib.ExitStack()
                with ff:
                    gu_ring = [sba(ff, f"gur{b}{i}", [128, 2048], BF16) for i in range(3)]
                    dn_ring = [sba(ff, f"dnr{b}0", [128, 6 * 1024], BF16), sba(ff, f"dnr{b}1", [128, 5 * 1024], BF16)]
                    gateT_bf = sba(ff, f"gateT{b}", [128, KC * NEXP], BF16)
                    seg_bf = sba(ff, f"seg{b}", [128, KC], BF16)
                    lnx = sba(ff, f"lnxf{b}", [128, KC * TH], BF16)
                    acc = sba(ff, f"acc{b}", [128, KC * TH], F32)
                    hbf = sba(ff, f"hbf{b}", [128, ICH * TH], BF16)
                    wbc = [sba(ff, f"wbc{b}{i}", [128, TH], BF16) for i in range(2)]
                    sigbc = sba(ff, f"sigbc{b}", [128, TH], BF16)
                    statf = mkstat(ff, f"f{b}", True)
                    sgt = [sba(ff, f"sgt{b}{i}", [128, 512], BF16) for i in range(2)]
                    uwt = [sba(ff, f"uwt{b}{i}", [128, 512], BF16) for i in range(2)]
                    g8 = {
                        "lg": sba(ff, f"lg{b}", [128, 8], F32),
                        "e8": sba(ff, f"e8{b}", [128, 8], F32),
                        "sc8": sba(ff, f"sc8{b}", [128, 8], F32),
                        "m": sba(ff, f"m8{b}", [128, 1], F32),
                        "ss": sba(ff, f"ss8{b}", [128, 1], F32),
                        "ri": sba(ff, f"ri8{b}", [128, 1], F32),
                        "msk": sba(ff, f"msk8{b}", [128, 8], F32),
                        "t8": sba(ff, f"t88{b}", [128, 8], F32),
                        "wst": sba(ff, f"wst{b}", [128, THC * 8], F32),
                        "wfb": sba(ff, f"wfb{b}", [8, TH], BF16),
                        "sgb": sba(ff, f"sgb{b}", [1, TH], BF16),
                    }
                    yst = [sba(ff, f"yst{b}{i}", [128, 512], F32) for i in range(2)]
                    pf = contextlib.ExitStack()
                    with pf:
                        pp = Pool([psa(pf, f"pp{b}{i}", [128, 512], F32) for i in range(8)])

                        P = new_phase()
                        P.g.op(lambda e: e.dma_start(out=gateT_bf[:], in_=ap["gate_i"]), dW, 16)
                        P.g.op(lambda e: e.dma_start(out=seg_bf[:], in_=ap["seg_i"]), dW, 16)
                        gate_w = dW.v

                        NGU = 3
                        gu_w = {}
                        gu_free = [None] * NGU
                        gu_last = {}
                        dn_w = {}
                        dn_free = [None] * 2

                        def gu_load(gidx):
                            p_, ic = divmod(gidx, ICH)
                            sl = gidx % NGU
                            if gu_free[sl] is not None:
                                P.g.wait(*gu_free[sl])
                            P.g.op(lambda e, p_=p_, ic=ic, sl=sl:
                                   e.dma_start(out=gu_ring[sl][:],
                                               in_=ap["wgu_i"][p_, :, ic * 2048:(ic + 1) * 2048]), dW, 16)
                            gu_w[gidx] = dW.v

                        def dn_load(didx):
                            p_, hk = divmod(didx, 2)
                            sl = didx % 2
                            ksz = 6 if hk == 0 else 5
                            if dn_free[sl] is not None:
                                P.g.wait(*dn_free[sl])
                            P.g.op(lambda e, p_=p_, hk=hk, sl=sl, ksz=ksz:
                                   e.dma_start(out=dn_ring[sl][:, 0:ksz * 1024],
                                               in_=ap["wdn_i"][p_, :, hk * 6 * 1024:(hk * 6 + ksz) * 1024]),
                                   dW, 16)
                            dn_w[didx] = dW.v

                        for tq in range(2):
                            t0 = tq * TH
                            for gidx in range(NGU):
                                gu_load(gidx)
                            for didx in range(2):
                                dn_load(didx)

                            xbfg = [None, None]
                            for tn in range(THN):
                                def srcf(kc, tn=tn):
                                    xb = statf["xbf"][kc % 2]
                                    if xbfg[kc % 2] is not None:
                                        P.v.wait(*xbfg[kc % 2])
                                    P.v.op(lambda e, kc=kc, xb=xb, tn=tn, t0=t0: e.tensor_copy(
                                        xb[:], resid[:, kc * S + t0 + tn * 512: kc * S + t0 + tn * 512 + 512]),
                                        cV)
                                    return xb[:], [(cV, cV.v)], kc % 2
                                stats_at = ln_chunk(P, pp, statf, srcf, xbfg)
                                P.v.wait(cV, stats_at)
                                for kc in range(KC):
                                    P.v.op(lambda e, kc=kc, tn=tn, t0=t0: e.tensor_tensor(
                                        out=statf["t1"][:],
                                        in0=resid[:, kc * S + t0 + tn * 512: kc * S + t0 + tn * 512 + 512],
                                        in1=statf["rinv"][:], op=ALU.mult))
                                    P.v.op(lambda e, kc=kc, tn=tn: e.tensor_tensor(
                                        out=lnx[:, kc * TH + tn * 512: kc * TH + tn * 512 + 512],
                                        in0=statf["t1"][:], in1=statf["mrinv"][:], op=ALU.subtract), cV)
                            lnx_at = cV.v

                            P.t.wait(dW, gate_w)
                            P.t.wait(cV, lnx_at)
                            for tc in range(THC):
                                s = pp.acquire(P.t)
                                for kc in range(KC):
                                    P.t.op(lambda e, kc=kc, tc=tc, ps=s["t"]:
                                           e.matmul(ps[:, 0:8],
                                                    lnx[:, kc * TH + tc * 128: kc * TH + (tc + 1) * 128],
                                                    gateT_bf[:, kc * 8:(kc + 1) * 8],
                                                    start=(kc == 0), stop=(kc == KC - 1)),
                                           cT if kc == KC - 1 else None)
                                P.v.wait(cT)
                                P.v.op(lambda e, ps=s["t"]: e.tensor_copy(g8["lg"][:], ps[:, 0:8]), cV)
                                s["free"] = (cV, cV.v)
                                P.v.wait(cV)
                                P.v.op(lambda e: e.reduce_max(g8["m"][:], g8["lg"][:], axis=AX, negate=True), cV)
                                P.s.wait(cV)
                                P.s.op(lambda e: e.activation(out=g8["e8"][:], in_=g8["lg"][:], func=AF.Exp,
                                                              bias=g8["m"][:], scale=1.0, accum_out=g8["ss"][:]),
                                       cS)
                                P.v.wait(cS)
                                P.v.op(lambda e: e.reciprocal(g8["ri"][:], g8["ss"][:]), cV)
                                P.v.wait(cV)
                                P.v.op(lambda e: e.tensor_scalar(out=g8["sc8"][:], in0=g8["e8"][:],
                                                                 scalar1=g8["ri"][:], scalar2=None, op0=ALU.mult), cV)
                                P.v.wait(cV)
                                P.v.op(lambda e: e.reduce_max(g8["m"][:], g8["sc8"][:], axis=AX, negate=True), cV)
                                P.v.wait(cV)
                                P.v.op(lambda e: e.tensor_scalar(out=g8["msk"][:], in0=g8["sc8"][:],
                                                                 scalar1=g8["m"][:], scalar2=0.0,
                                                                 op0=ALU.add, op1=ALU.is_ge), cV)
                                P.v.wait(cV)
                                P.v.op(lambda e: e.scalar_tensor_tensor(out=g8["t8"][:], in0=g8["msk"][:],
                                                                        scalar=-1e30, in1=g8["sc8"][:],
                                                                        op0=ALU.mult, op1=ALU.add), cV)
                                P.v.wait(cV)
                                P.v.op(lambda e: e.reduce_max(g8["m"][:], g8["t8"][:], axis=AX, negate=True), cV)
                                P.v.wait(cV)
                                P.v.op(lambda e: e.tensor_scalar(out=g8["msk"][:], in0=g8["sc8"][:],
                                                                 scalar1=g8["m"][:], scalar2=0.0,
                                                                 op0=ALU.add, op1=ALU.is_ge), cV)
                                P.v.wait(cV)
                                P.v.op(lambda e, tc=tc: e.tensor_tensor(out=g8["wst"][:, tc * 8:(tc + 1) * 8],
                                                                        in0=g8["sc8"][:], in1=g8["msk"][:],
                                                                        op=ALU.mult), cV)
                            wst_at = cV.v
                            for gtc in range(THC // 4):
                                s = pp.acquire(P.t)
                                P.t.wait(cV, wst_at)
                                for j in range(4):
                                    tc = gtc * 4 + j
                                    P.t.op(lambda e, tc=tc, j=j, ps=s["t"]:
                                           e.transpose(ps[0:8, j * 128:(j + 1) * 128],
                                                       g8["wst"][:, tc * 8:(tc + 1) * 8], cid_f[:]),
                                           cT if j == 3 else None)
                                P.v.wait(cT)
                                P.v.op(lambda e, gtc=gtc, ps=s["t"]:
                                       e.tensor_copy(g8["wfb"][0:8, gtc * 512:(gtc + 1) * 512], ps[0:8, 0:512]),
                                       cV)
                                s["free"] = (cV, cV.v)
                            wfb_at = cV.v

                            for tn in range(THN):
                                s = pp.acquire(P.t)
                                for kc in range(KC):
                                    P.t.op(lambda e, kc=kc, tn=tn, ps=s["t"]:
                                           e.matmul(ps[0:1, 0:512], seg_bf[:, kc:kc + 1],
                                                    lnx[:, kc * TH + tn * 512: kc * TH + tn * 512 + 512],
                                                    start=(kc == 0), stop=(kc == KC - 1)),
                                           cT if kc == KC - 1 else None)
                                P.s.wait(cT)
                                P.s.op(lambda e, tn=tn, ps=s["t"]:
                                       e.activation(out=g8["sgb"][0:1, tn * 512:(tn + 1) * 512],
                                                    in_=ps[0:1, 0:512], func=AF.Sigmoid), cS)
                                s["free"] = (cS, cS.v)
                            sgb_at = None
                            for tn in range(THN):
                                s = pp.acquire(P.t)
                                P.t.wait(cS)
                                P.t.op(lambda e, tn=tn, ps=s["t"]:
                                       e.matmul(ps[:], ones_bf[0:1, :], g8["sgb"][0:1, tn * 512:(tn + 1) * 512],
                                                start=True, stop=True), cT)
                                P.v.wait(cT)
                                P.v.op(lambda e, tn=tn, ps=s["t"]:
                                       e.tensor_copy(sigbc[:, tn * 512:(tn + 1) * 512], ps[:]), cV)
                                s["free"] = (cV, cV.v)
                            if b == 0 and tq == 0:
                                P.emit(block)
                                dump_and_stop("router", {"lnxf": lnx[:], "wst": g8["wst"][:],
                                                         "wf": g8["wfb"][:], "sigbc": sigbc[:],
                                                         "resid": resid[:],
                                                         "lg": g8["lg"][:], "sc8": g8["sc8"][:],
                                                         "msk": g8["msk"][:], "t8": g8["t8"][:]})
                                if stop in ("router",):
                                    raise _StopBuild()
                                P = new_phase()

                            hbf_guard = [None]
                            sgt_free = [None, None]
                            wbc_at = [0, 0]
                            for p_ in range(10):
                                if p_ < 8:
                                    for tn in range(THN):
                                        s = pp.acquire(P.t)
                                        P.t.wait(cV, wfb_at)
                                        P.t.op(lambda e, p_=p_, tn=tn, ps=s["t"]:
                                               e.matmul(ps[:], csel_sb[0:8, p_ * 128:(p_ + 1) * 128],
                                                        g8["wfb"][0:8, tn * 512:(tn + 1) * 512],
                                                        start=True, stop=True), cT)
                                        P.v.wait(cT)
                                        P.v.op(lambda e, p_=p_, tn=tn, ps=s["t"]:
                                               e.tensor_copy(wbc[p_ % 2][:, tn * 512:(tn + 1) * 512], ps[:]),
                                               cV)
                                        s["free"] = (cV, cV.v)
                                    wbc_at[p_ % 2] = cV.v
                                    wsel = wbc[p_ % 2]
                                    wsel_at = wbc_at[p_ % 2]
                                else:
                                    wsel = sigbc
                                    wsel_at = cV.v

                                for ic in range(ICH):
                                    gidx = (p_ * ICH + ic)
                                    sl = gidx % NGU
                                    P.t.wait(dW, gu_w[gidx])
                                    ps_g = [None, None]
                                    ps_u = [None, None]
                                    for gu in range(2):
                                        for tn in range(THN):
                                            s = pp.acquire(P.t)
                                            if hbf_guard[0] is not None and ic == 0 and gu == 0:
                                                pass
                                            for kc in range(KC):
                                                P.t.op(lambda e, gu=gu, kc=kc, tn=tn, sl=sl, ps=s["t"]:
                                                       e.matmul(ps[:],
                                                                gu_ring[sl][:, gu * 1024 + kc * 128: gu * 1024 + (kc + 1) * 128],
                                                                lnx[:, kc * TH + tn * 512: kc * TH + tn * 512 + 512],
                                                                start=(kc == 0), stop=(kc == KC - 1)),
                                                       cT if kc == KC - 1 else None)
                                            if gu == 0:
                                                ps_g[tn] = (s, cT.v)
                                            else:
                                                ps_u[tn] = (s, cT.v)
                                    gu_free[sl] = (cT, cT.v)
                                    if gidx + NGU < 10 * ICH:
                                        gu_load(gidx + NGU)
                                    silu_vs = [None] * THN
                                    for tn in range(THN):
                                        sG, gv = ps_g[tn]
                                        if sgt_free[tn] is not None:
                                            P.s.wait(*sgt_free[tn])
                                        P.s.wait(cT, gv)
                                        P.s.op(lambda e, tn=tn, ps=sG["t"]:
                                               e.activation(out=sgt[tn][:], in_=ps[:], func=AF.Silu), cS)
                                        sG["free"] = (cS, cS.v)
                                        silu_vs[tn] = cS.v
                                    for tn in range(THN):
                                        sU, uv = ps_u[tn]
                                        P.v.wait(cT, uv)
                                        P.v.wait(cV, wsel_at)
                                        P.v.op(lambda e, tn=tn, ps=sU["t"], wsel=wsel:
                                               e.tensor_tensor(out=uwt[tn][:], in0=ps[:],
                                                               in1=wsel[:, tn * 512:(tn + 1) * 512],
                                                               op=ALU.mult), cV)
                                        sU["free"] = (cV, cV.v)
                                    uw_done = cV.v
                                    P.v.wait(cV, uw_done) if THN == 1 else None
                                    for tn in range(THN):
                                        if hbf_guard[0] is not None:
                                            P.v.wait(*hbf_guard[0])
                                        P.v.wait(cS, silu_vs[tn])
                                        if THN == 1:
                                            P.v.wait(cV, uw_done)
                                        P.v.op(lambda e, ic=ic, tn=tn:
                                               e.tensor_tensor(out=hbf[:, ic * TH + tn * 512: ic * TH + tn * 512 + 512],
                                                               in0=sgt[tn][:], in1=uwt[tn][:], op=ALU.mult), cV)
                                        sgt_free[tn] = (cV, cV.v)
                                h_at = cV.v

                                for hk in range(2):
                                    didx = p_ * 2 + hk
                                    sl = didx % 2
                                    ksz = 6 if hk == 0 else 5
                                    kbase = hk * 6
                                    P.t.wait(dW, dn_w[didx])
                                    P.t.wait(cV, h_at)
                                    for hc in range(KC):
                                        for tn in range(THN):
                                            s = pp.acquire(P.t)
                                            for j in range(ksz):
                                                kc = kbase + j
                                                P.t.op(lambda e, j=j, kc=kc, hc=hc, tn=tn, sl=sl, ksz=ksz, ps=s["t"]:
                                                       e.matmul(ps[:],
                                                                dn_ring[sl][:, j * 1024 + hc * 128: j * 1024 + (hc + 1) * 128],
                                                                hbf[:, kc * TH + tn * 512: kc * TH + tn * 512 + 512],
                                                                start=(j == 0), stop=(j == ksz - 1)),
                                                       cT if j == ksz - 1 else None)
                                            P.v.wait(cT)
                                            if p_ == 0 and hk == 0:
                                                P.v.op(lambda e, hc=hc, tn=tn, ps=s["t"]:
                                                       e.tensor_copy(acc[:, hc * TH + tn * 512: hc * TH + tn * 512 + 512],
                                                                     ps[:]), cV)
                                            else:
                                                P.v.op(lambda e, hc=hc, tn=tn, ps=s["t"]:
                                                       e.scalar_tensor_tensor(
                                                           out=acc[:, hc * TH + tn * 512: hc * TH + tn * 512 + 512],
                                                           in0=ps[:], scalar=1.0,
                                                           in1=acc[:, hc * TH + tn * 512: hc * TH + tn * 512 + 512],
                                                           op0=ALU.mult, op1=ALU.add), cV)
                                            s["free"] = (cV, cV.v)
                                    dn_free[sl] = (cT, cT.v)
                                    if didx + 2 < 20:
                                        dn_load(didx + 2)
                                hbf_guard[0] = (cT, cT.v)
                                if b == 0 and tq == 0 and stop == f"exp{p_}":
                                    P.emit(block)
                                    dump_and_stop(f"exp{p_}", {"hbf": hbf[:], "acc": acc[:],
                                                               "wbc0": wbc[0][:], "lnxf": lnx[:],
                                                               "sgt0": sgt[0][:], "uwt0": uwt[0][:],
                                                               "cselt": csel_sb[:], "wfbt": g8["wfb"][:]})
                                    P = new_phase()
                            acc_at = cV.v
                            if b == 0 and tq == 0:
                                P.emit(block)
                                dump_and_stop("acc", {"acc": acc[:], "resid": resid[:]})
                                P = new_phase()

                            xbfg4 = [None, None]
                            for tn in range(THN):
                                def src4(kc, tn=tn):
                                    xb = statf["xbf"][kc % 2]
                                    if xbfg4[kc % 2] is not None:
                                        P.v.wait(*xbfg4[kc % 2])
                                    P.v.wait(cV, acc_at)
                                    P.v.op(lambda e, kc=kc, xb=xb, tn=tn: e.tensor_copy(
                                        xb[:], acc[:, kc * TH + tn * 512: kc * TH + tn * 512 + 512]), cV)
                                    return xb[:], [(cV, cV.v)], kc % 2
                                stats_at = ln_chunk(P, pp, statf, src4, xbfg4)
                                P.v.wait(cV, stats_at)
                                for kc in range(KC):
                                    P.v.op(lambda e, kc=kc, tn=tn: e.tensor_tensor(
                                        out=statf["t1"][:],
                                        in0=acc[:, kc * TH + tn * 512: kc * TH + tn * 512 + 512],
                                        in1=statf["rinv"][:], op=ALU.mult))
                                    P.v.op(lambda e: e.tensor_tensor(
                                        out=statf["var"][:], in0=statf["t1"][:], in1=statf["mrinv"][:],
                                        op=ALU.subtract))
                                    P.v.op(lambda e, kc=kc, tn=tn, t0=t0: e.tensor_tensor(
                                        out=resid[:, kc * S + t0 + tn * 512: kc * S + t0 + tn * 512 + 512],
                                        in0=resid[:, kc * S + t0 + tn * 512: kc * S + t0 + tn * 512 + 512],
                                        in1=statf["var"][:], op=ALU.add), cV)
                            resid_at = cV.v

                            yst_guard = [None, None]
                            for tc in range(THC):
                                tcg = tq * THC + tc
                                for g2 in range(2):
                                    s = pp.acquire(P.t)
                                    P.t.wait(cV, resid_at)
                                    for dc in range(4):
                                        dcg = g2 * 4 + dc
                                        P.t.op(lambda e, dcg=dcg, dc=dc, tcg=tcg, ps=s["t"]:
                                               e.transpose(ps[:, dc * 128:(dc + 1) * 128],
                                                           resid[:, dcg * S + tcg * 128: dcg * S + (tcg + 1) * 128],
                                                           cid_f[:]),
                                               cT if dc == 3 else None)
                                    P.v.wait(cT)
                                    if yst_guard[g2] is not None:
                                        P.v.wait(*yst_guard[g2])
                                    P.v.op(lambda e, g2=g2, ps=s["t"]:
                                           e.tensor_copy(yst[g2][:], ps[:]), cV)
                                    s["free"] = (cV, cV.v)
                                    P.y.wait(cV)
                                    P.y.op(lambda e, g2=g2, tcg=tcg:
                                           e.dma_start(out=y_out[b, tcg * 128:(tcg + 1) * 128,
                                                                 g2 * 512:(g2 + 1) * 512],
                                                       in_=yst[g2][:]), dOUT, 16)
                                    yst_guard[g2] = (dOUT, dOUT.v)
                        P.emit(block)

        phF = Phase()
        phF.y.wait(dOUT, dOUT.v)
        phF.emit(block)
        import os
        if os.environ.get("K_PRINT_SBUF"):
            print("min sbuf remaining (bytes/partition):", minrem[0])
    except _StopBuild:
        pass
    return nc


_CACHE = {}


def kernel(L0_attn_x, L0_attn_Wq, L0_attn_Wkv, L0_attn_Wo, L0_attn_bo, L0_attn_rel_emb,
           L0_ffn_gate_w, L0_ffn_experts_gate_up, L0_ffn_experts_down,
           L0_ffn_shared_gate_w, L0_ffn_shared_up_w, L0_ffn_shared_down_w,
           L0_ffn_shared_expert_gate_w):
    x = np.ascontiguousarray(np.asarray(L0_attn_x, dtype=np.float32))
    B, S, _ = x.shape
    BPC = B // NCORES

    lr_i, LRP = _build_lr(np.asarray(L0_attn_rel_emb, np.float32), S)
    wq_i = _img_lhsT(np.asarray(L0_attn_Wq, np.float32))
    wkv_i = _img_lhsT(np.asarray(L0_attn_Wkv, np.float32))
    wo_i = _img_lhsT(np.asarray(L0_attn_Wo, np.float32))
    bo_i = np.ascontiguousarray(np.asarray(L0_attn_bo, np.float32).reshape(8, 128).T)
    gate_i = _img_lhsT(np.ascontiguousarray(np.asarray(L0_ffn_gate_w, np.float32).T))
    seg_i = _img_lhsT(np.ascontiguousarray(np.asarray(L0_ffn_shared_expert_gate_w, np.float32).T))

    egu = np.asarray(L0_ffn_experts_gate_up, np.float32)
    edn = np.asarray(L0_ffn_experts_down, np.float32)
    sg = np.asarray(L0_ffn_shared_gate_w, np.float32)
    su = np.asarray(L0_ffn_shared_up_w, np.float32)
    sd = np.asarray(L0_ffn_shared_down_w, np.float32)

    IEXP_ = ICH * 128
    wgu_list = [_img_gu(egu[e, :IEXP_], egu[e, IEXP_:]) for e in range(NEXP)]
    wgu_list.append(_img_gu(sg[:IEXP_], su[:IEXP_]))
    wgu_list.append(_img_gu(sg[IEXP_:], su[IEXP_:]))
    wgu_i = np.ascontiguousarray(np.stack(wgu_list, axis=0))
    wdn_list = [_img_down(edn[e]) for e in range(NEXP)]
    wdn_list.append(_img_down(sd[:, :IEXP_]))
    wdn_list.append(_img_down(sd[:, IEXP_:]))
    wdn_i = np.ascontiguousarray(np.stack(wdn_list, axis=0))

    cid_f = np.eye(128, dtype=np.float32)
    ceps = np.full((128, 1), 1e-5, np.float32)
    csel = np.zeros((8, 8 * 128), np.float32)
    for e in range(8):
        csel[e, e * 128:(e + 1) * 128] = 1.0
    csel = csel.astype(BF)
    cmisc = np.concatenate([np.eye(128, dtype=np.float32),
                            np.eye(128, dtype=np.float32) * 8.0,
                            np.ones((128, 128), np.float32)], axis=1).astype(BF)

    key = (S, BPC)
    if key not in _CACHE:
        _CACHE[key] = build_program(S, BPC, LRP)
    nc = _CACHE[key]

    common = dict(wq_i=wq_i, wkv_i=wkv_i, wo_i=wo_i, bo_i=bo_i, gate_i=gate_i, seg_i=seg_i,
                  lr_i=lr_i, wgu_i=wgu_i, wdn_i=wdn_i, cid_f=cid_f, cmisc=cmisc, ceps=ceps, csel=csel)
    in_maps = [dict(common, x=np.ascontiguousarray(x[c * BPC:(c + 1) * BPC]))
               for c in range(NCORES)]
    res = run_bass_kernel_spmd(nc, in_maps, core_ids=list(range(NCORES)))
    return np.concatenate([r["y"] for r in res.results], axis=0)
